# version 21
# speedup vs baseline: 1.0698x; 1.0025x over previous
import numpy as np
from contextlib import ExitStack

import concourse.bass as bass
import concourse.mybir as mybir
from concourse.bass_utils import run_bass_kernel_spmd

F32 = mybir.dt.float32
BF16 = mybir.dt.bfloat16
AF = mybir.ActivationFunctionType
ALU = mybir.AluOpType
AX = mybir.AxisListType

NCORE = 8
BSEQ = 2
S = 2048
C = 1024
KC = 8
NH = 16
HD = 64
NU = 12
NIC = 4
TT = 512
NTILE = 8
NG = 32
TPC = BSEQ * S
CAP = 5120.0
LNEPS = 1e-5
ATTN_SCALE = 0.125

_DEBUG = set()
_STOP = None

ENGS = ("sync", "gpsimd", "tensor", "vector", "scalar")


class Bld:

    def __init__(self, nc):
        self.nc = nc
        self.sems = {}
        self.val = {}
        self.waited = {e: {} for e in ENGS}
        self.ops = None
        self.queue_sem = {"sync": "qS", "gpsimd": "qG", "scalar": "qA"}

    def add_sem(self, name, handle):
        self.sems[name] = handle
        self.val[name] = 0

    def phase(self):
        self.ops = {e: [] for e in ENGS}

    def op(self, eng, fn, wait=(), inc=None, n=1):
        w = tuple(t for t in wait if t is not None)
        tok = None
        if inc is not None:
            self.val[inc] += n
            tok = (inc, self.val[inc])
        self.ops[eng].append((w, fn, inc, n))
        return tok

    def dma(self, eng, out, in_, wait=()):
        q = self.queue_sem[eng]
        return self.op(eng, lambda e, o=out, i=in_: e.dma_start(out=o, in_=i),
                       wait, inc=q, n=16)

    def t(self, fn, wait=()):
        return self.op("tensor", fn, wait, inc="tE")

    serial_v = False

    def v(self, fn, wait=()):
        tok = self.op("vector", fn, wait, inc="vE")
        if self.serial_v:
            self.ops["vector"].append(((tok,), None, None, 0))
        return tok

    def a(self, fn, wait=()):
        return self.op("scalar", fn, wait, inc="aE")

    def g(self, fn, wait=()):
        return self.op("gpsimd", fn, wait, inc="pE")

    def vmemset(self, ap, val):
        tok = self.v(lambda v, a=ap, c=val: v.memset(a, c))
        self.ops["vector"].append(((tok,), None, None, 0))
        return tok

    def mm_group(self, bank_fn, ops, wait_first=()):
        n = len(ops)
        tok = None
        for i, (lhsT, rhs) in enumerate(ops):
            w = wait_first if i == 0 else ()
            tok = self.op(
                "tensor",
                lambda t, bf=bank_fn, l=lhsT, r=rhs, st=(i == 0),
                sp=(i == n - 1): t.matmul(bf(), l, r, start=st, stop=sp,
                                          skip_group_check=True),
                w, inc="tE" if i == n - 1 else None)
        return tok

    def flush(self, skip_q=()):
        nc = self.nc
        for eng, q in self.queue_sem.items():
            if q in skip_q:
                continue
            if self.val.get(q, 0) > self.waited[eng].get(q, 0) and \
                    any(i == q for (_, _, i, _) in self.ops[eng]):
                self.ops[eng].append((((q, self.val[q]),), None, None, 0))
        with nc.Block() as block:
            for eng in ENGS:
                lst = self.ops[eng]
                if not lst:
                    continue
                getattr(block, eng)(self._body(eng, lst))
        self.ops = None

    def _body(self, eng, lst):
        def body(e):
            for (waits, fn, inc, n) in lst:
                for (sname, v) in waits:
                    if self.waited[eng].get(sname, 0) < v:
                        e.wait_ge(self.sems[sname], v)
                        self.waited[eng][sname] = v
                if fn is None:
                    continue
                inst = fn(e)
                if inc is not None:
                    inst.then_inc(self.sems[inc], n)
        return body


def fm_tile(ap3, s, j, w=TT):
    return ap3[s].rearrange("(kc p) t -> p kc t", p=128)[:, :, j * w:(j + 1) * w]


def build_program():
    nc = bass.Bass(target_bir_lowering=False)

    def dram(name, shape, dtype, kind=None):
        if kind is None and name in _DEBUG:
            kind = "ExternalOutput"
        if kind is None:
            return nc.dram_tensor(name, shape, dtype).ap()
        return nc.dram_tensor(name, shape, dtype, kind=kind).ap()

    x_ap = dram("x", [BSEQ, C, S], F32, "ExternalInput")
    wg_ap = dram("wg", [NU, C, 512], F32, "ExternalInput")
    wu_ap = dram("wu", [NU, C, 512], F32, "ExternalInput")
    wd_ap = dram("wd", [NU, 512, C], F32, "ExternalInput")
    wq4_ap = dram("wqkvo", [4, C, C], F32, "ExternalInput")
    gwt_ap = dram("gwt", [C, 9], F32, "ExternalInput")
    cid_ap = dram("cid", [128, 1], F32, "ExternalInput")
    kco_ap = dram("kco", [12, NU, 128], BF16, "ExternalInput")
    ktri_ap = dram("ktri", [128, 128], F32, "ExternalInput")
    kid_ap = dram("kid", [128, 128], F32, "ExternalInput")
    kpre_ap = dram("kpre", [8, 128], F32, "ExternalInput")
    out_ap = dram("out", [BSEQ, C, S], F32, "ExternalOutput")

    x1_ap = dram("x1", [BSEQ, C, S], F32)
    x2_ap = dram("x2", [BSEQ, C, S], F32)
    xln_ap = dram("xln", [BSEQ, C, S], BF16)
    cwg_ap = dram("cwg", [NU, C, 512], BF16)
    cwu_ap = dram("cwu", [NU, C, 512], BF16)
    cwd_ap = dram("cwd", [NU, 512, C], BF16)
    cwq_ap = dram("cwq", [4, C, C], BF16)
    dbg = {}
    for nm, shp, dt in (("dbg_wexp", [12, TPC], BF16),
                        ("dbg_log", [128, NG, 9], F32),
                        ("dbg_rwn", [128, NG, 8], F32),
                        ("dbg_pre", [128, NG, 8], F32),
                        ("dbg_kept", [128, NG, 8], F32),
                        ("dbg_thr", [128, 4, NG], F32),
                        ("dbg_D", [128, 4, NG], F32),
                        ("dbg_e1", [128, NG, 8], F32),
                        ("dbg_cf", [128, NG, 4, 8], F32),
                        ("dbg_val", [128, NG, 4], F32),
                        ("dbg_smv", [128, 4, NG], F32),
                        ("dbg_stc", [128, NG, 2], F32),
                        ("dbg_srw", [128, NG, 8], F32),
                        ("dbg_se4", [128, 4, NG], F32),
                        ("dbg_se4b", [128, 4, NG], F32),
                        ("dbg_a", [128, NU, NIC, TT], BF16),
                        ("dbg_x1t", [128, KC, TT], F32)):
        if nm in _DEBUG:
            dbg[nm] = dram(nm, shp, dt, "ExternalOutput")
    ccin_ap = [dram("ccin%d" % i, [1, 8], F32) for i in range(2)]
    ccout_ap = [dram("ccout%d" % i, [NCORE, 8], F32) for i in range(2)]

    B = Bld(nc)

    with ExitStack() as top:
        et = top.enter_context
        for sname in ("qS", "qG", "qA", "tE", "vE", "aE", "pE"):
            B.add_sem(sname, et(nc.semaphore(sname)))

        ones_bf = et(nc.sbuf_tensor("ones_bf", [128, 128], BF16))
        cmneg = et(nc.sbuf_tensor("cmneg", [128, 4, TT], BF16))
        idb = et(nc.sbuf_tensor("idb", [128, 128], BF16))
        s8e = et(nc.sbuf_tensor("s8e", [128, 8], F32))
        scid = et(nc.sbuf_tensor("scid", [128, 1], F32))
        scc = et(nc.sbuf_tensor("scc", [8, 8], F32))
        seps = et(nc.sbuf_tensor("seps", [128, 1], F32))

        B.phase()
        with ExitStack() as c0:
            e0 = c0.enter_context
            it_p = e0(nc.sbuf_tensor("it_p", [128, TT], F32))
            it_f = e0(nc.sbuf_tensor("it_f", [128, TT], F32))
            sdm = e0(nc.sbuf_tensor("sdm", [128, TT], F32))
            wstg0 = e0(nc.sbuf_tensor("wstg0", [128, 2, KC, 1024], BF16))

            tcid = B.dma("sync", scid[:], cid_ap)
            tip = B.g(lambda g: g.iota(it_p[:], [[0, TT]], channel_multiplier=1,
                                       allow_small_or_imprecise_dtypes=True))
            tif = B.g(lambda g: g.iota(it_f[:], [[1, TT]], channel_multiplier=0,
                                       allow_small_or_imprecise_dtypes=True))
            tie = B.g(lambda g: g.iota(s8e[:], [[1, 8]], channel_multiplier=0,
                                       allow_small_or_imprecise_dtypes=True))
            B.vmemset(seps[:], LNEPS)
            B.vmemset(ones_bf[:], 1.0)
            B.v(lambda v: v.tensor_tensor(idb[:], it_p[:, 0:128],
                                          it_f[:, 0:128], ALU.is_equal),
                wait=(tip, tif))
            B.v(lambda v: v.tensor_sub(sdm[:], it_f[:], it_p[:]))
            for d in range(4):
                B.v(lambda v, d=d: v.tensor_single_scalar(
                    cmneg[:, d, :], sdm[:], float(128 * d), ALU.is_lt))
                B.v(lambda v, d=d: v.tensor_single_scalar(
                    cmneg[:, d, :], cmneg[:, d, :], -800.0, ALU.mult))
            B.v(lambda v: v.tensor_single_scalar(s8e[:], s8e[:], -1.0, ALU.mult),
                wait=(tie,))
            B.v(lambda v: v.tensor_single_scalar(s8e[:], s8e[:], 8.0, ALU.add))
            relw = [None, None]
            for m in range(4):
                b = m % 2
                tl = B.dma("gpsimd", wstg0[:, b, :, :],
                           wq4_ap[m].rearrange("(kc p) o -> p kc o", p=128),
                           wait=(relw[b],))
                relw[b] = B.dma("gpsimd",
                                cwq_ap[m].rearrange("(kc p) o -> p kc o", p=128),
                                wstg0[:, b, :, :], wait=(tl,))
            B.flush()

        def emit_route(ctx, tag, src_ap, build_wcache, cc_i):
            e1 = ctx.enter_context
            slog9 = e1(nc.sbuf_tensor(tag + "slog9", [128, NG, 9], F32))
            sl8 = e1(nc.sbuf_tensor(tag + "sl8", [128, NG], F32))
            sgw = e1(nc.sbuf_tensor(tag + "sgw", [128, KC, 9], F32))
            srwn = e1(nc.sbuf_tensor(tag + "srwn", [128, NG, 8], F32))
            spre = e1(nc.sbuf_tensor(tag + "spre", [128, NG, 8], F32))
            wstg = e1(nc.sbuf_tensor(tag + "wstg", [128, 2, KC, TT],
                                     BF16)) if build_wcache else None
            onehot = e1(nc.sbuf_tensor(tag + "oh", [12, NU, 128], BF16))
            tri_f = e1(nc.sbuf_tensor(tag + "tri", [128, 128], F32))
            idf = e1(nc.sbuf_tensor(tag + "idf", [128, 128], F32))
            prefm = e1(nc.sbuf_tensor(tag + "pre", [8, 128], F32))
            ones_f = e1(nc.sbuf_tensor(tag + "onef", [128, 128], F32))
            B.phase()
            with ExitStack() as cl:
                el = cl.enter_context
                sxt = el(nc.sbuf_tensor(tag + "sxt", [128, 2, KC, TT], F32))
                sxb = el(nc.sbuf_tensor(tag + "sxb", [128, 2, KC, TT], BF16))
                sxq = el(nc.sbuf_tensor(tag + "sxq", [128, 2, KC, TT], BF16))
                sxln32 = el(nc.sbuf_tensor(tag + "sl32", [128, 2, KC, TT], F32))
                sxlnb = el(nc.sbuf_tensor(tag + "slb", [128, 2, KC, TT], BF16))
                sstat = el(nc.sbuf_tensor(tag + "stat", [128, 2, 4, TT], F32))
                std_ = el(nc.sbuf_tensor(tag + "std", [128, 2, TT], F32))
                p8 = el(nc.psum_tensor(tag + "p8r", [128, 8, TT], F32))

                tgw = B.dma("sync", sgw[:],
                            gwt_ap.rearrange("(kc p) e -> p kc e", p=128))
                B.dma("sync", onehot[:], kco_ap)
                B.dma("sync", tri_f[:], ktri_ap)
                B.dma("sync", idf[:], kid_ap)
                B.dma("sync", prefm[:], kpre_ap)
                B.kconsts = (onehot, tri_f, idf, prefm, ones_f)
                B.vmemset(ones_f[:], 1.0)
                rel_buf = [None, None]
                rel_lnb = [None, None]
                rel_wc = [None, None]
                wc_tok = {}
                btok = {}
                for tile in range(NTILE):
                    sq, j = divmod(tile, 4)
                    bf = tile % 2
                    tx = B.dma("sync", sxt[:, bf, :, :], fm_tile(src_ap, sq, j),
                               wait=(rel_buf[bf],))
                    txb = B.v(lambda v, bf=bf: v.tensor_copy(
                        sxb[:, bf, :, :], sxt[:, bf, :, :]), wait=(tx,))
                    txq = B.v(lambda v, bf=bf: v.tensor_tensor(
                        sxq[:, bf, :, :], sxt[:, bf, :, :], sxt[:, bf, :, :],
                        ALU.mult))
                    tm = B.mm_group(
                        lambda bf=bf: p8[:, 2 * bf, :],
                        [(ones_bf[:], sxb[:, bf, kc, :]) for kc in range(KC)],
                        wait_first=(txb, btok.get((bf, "sA"))))
                    tq = B.mm_group(
                        lambda bf=bf: p8[:, 2 * bf + 1, :],
                        [(ones_bf[:], sxq[:, bf, kc, :]) for kc in range(KC)],
                        wait_first=(txq, btok.get((bf, "sB"))))
                    btok[(bf, "sA")] = B.v(
                        lambda v, bf=bf: v.tensor_single_scalar(
                            sstat[:, bf, 0, :], p8[:, 2 * bf, :], 1.0 / C,
                            ALU.mult), wait=(tm,))
                    btok[(bf, "sB")] = B.v(
                        lambda v, bf=bf: v.tensor_single_scalar(
                            sstat[:, bf, 1, :], p8[:, 2 * bf + 1, :], 1.0 / C,
                            ALU.mult), wait=(tq,))
                    B.v(lambda v, bf=bf: v.scalar_tensor_tensor(
                        sstat[:, bf, 2, :], sstat[:, bf, 0, :], -1.0,
                        sstat[:, bf, 0, :], ALU.mult, ALU.mult))
                    t4 = B.v(lambda v, bf=bf: v.tensor_add(
                        sstat[:, bf, 2, :], sstat[:, bf, 2, :],
                        sstat[:, bf, 1, :]))
                    B.a(lambda a, bf=bf: a.activation(
                        sstat[:, bf, 3, :], sstat[:, bf, 2, :], AF.Ln,
                        bias=seps[:], scale=1.0), wait=(t4,))
                    t6 = B.a(lambda a, bf=bf: a.activation(
                        sstat[:, bf, 2, :], sstat[:, bf, 3, :], AF.Exp,
                        bias=0.0, scale=-0.5))
                    tn = None
                    for kc in range(KC):
                        B.v(lambda v, bf=bf, kc=kc: v.tensor_sub(
                            std_[:, bf, :], sxt[:, bf, kc, :],
                            sstat[:, bf, 0, :]),
                            wait=(t6,) if kc == 0 else ())
                        tn = B.v(lambda v, bf=bf, kc=kc: v.tensor_tensor(
                            sxln32[:, bf, kc, :], std_[:, bf, :],
                            sstat[:, bf, 2, :], ALU.mult))
                    tb = B.v(lambda v, bf=bf: v.tensor_copy(
                        sxlnb[:, bf, :, :], sxln32[:, bf, :, :]),
                        wait=(rel_lnb[bf],))
                    rel_lnb[bf] = B.dma("sync", fm_tile(xln_ap, sq, j),
                                        sxlnb[:, bf, :, :], wait=(tb,))
                    last_pl = None
                    for tc in range(4):
                        gidx = tile * 4 + tc
                        pb = 4 + (gidx % 2)
                        tlg = B.mm_group(
                            lambda pb=pb: p8[:, pb, 0:9],
                            [(sxln32[:, bf, kc, tc * 128:(tc + 1) * 128],
                              sgw[:, kc, :]) for kc in range(KC)],
                            wait_first=(tn, tgw, btok.get(("pl", pb))))
                        B.v(lambda v, gidx=gidx, pb=pb: v.tensor_copy(
                            slog9[:, gidx, :], p8[:, pb, 0:9]), wait=(tlg,))
                        last_pl = B.v(
                            lambda v, gidx=gidx, pb=pb: v.tensor_copy(
                                sl8[:, gidx:gidx + 1], p8[:, pb, 8:9]))
                        btok[("pl", pb)] = last_pl
                    rel_buf[bf] = last_pl

                    if build_wcache and tile < 6:
                        for k in range(2):
                            e = 2 * tile + k
                            wb = (2 * tile + k) % 2
                            for (src, dst, nch, ncol) in (
                                    (wg_ap, cwg_ap, 8, 512),
                                    (wu_ap, cwu_ap, 8, 512),
                                    (wd_ap, cwd_ap, 4, 1024)):
                                stg = wstg[:, wb, :, :].rearrange(
                                    "p a b -> p (a b)").rearrange(
                                    "p (c d) -> p c d", d=ncol)[:, 0:nch, :]
                                tl = B.dma("gpsimd", stg,
                                           src[e].rearrange("(a p) b -> p a b",
                                                            p=128),
                                           wait=(rel_wc[wb],))
                                rel_wc[wb] = B.dma(
                                    "gpsimd",
                                    dst[e].rearrange("(a p) b -> p a b", p=128),
                                    stg, wait=(tl,))
                                wc_tok[(dst is cwg_ap, dst is cwu_ap, e)] = \
                                    rel_wc[wb]

                B.serial_v = True
                smsk = el(nc.sbuf_tensor(tag + "smsk", [128, NG, 8], F32))
                se1 = el(nc.sbuf_tensor(tag + "se1", [128, NG, 8], F32))
                sno = el(nc.sbuf_tensor(tag + "sno", [128, NG, 8], F32))
                scf = el(nc.sbuf_tensor(tag + "scf", [128, NG, 4, 8], F32))
                sthr = el(nc.sbuf_tensor(tag + "sthr", [128, 4, NG], F32))
                sD = el(nc.sbuf_tensor(tag + "sD", [128, 4, NG], F32))
                se4 = el(nc.sbuf_tensor(tag + "se4", [128, 4, NG], F32))
                sS3 = el(nc.sbuf_tensor(tag + "sS3", [128, NG, 3], F32))
                sval = el(nc.sbuf_tensor(tag + "sval", [128, NG, 4], F32))
                smv = el(nc.sbuf_tensor(tag + "smv", [128, 4, NG], F32))
                srwA = el(nc.sbuf_tensor(tag + "srwA", [128, NG, 8], F32))
                st_a = el(nc.sbuf_tensor(tag + "st_a", [128, NG, 8], F32))
                st_b = el(nc.sbuf_tensor(tag + "st_b", [128, NG, 8], F32))
                st_c = el(nc.sbuf_tensor(tag + "st_c", [128, NG, 2], F32))
                sabs = el(nc.sbuf_tensor(tag + "sabs", [128, NG, 8], F32))
                sZ8 = el(nc.sbuf_tensor(tag + "sZ8", [128, NG], F32))
                szc = el(nc.sbuf_tensor(tag + "szc", [128, NG], F32))
                samx = el(nc.sbuf_tensor(tag + "samx", [128, NG], F32))
                sthn = el(nc.sbuf_tensor(tag + "sthn", [128, NG], F32))
                sthab = el(nc.sbuf_tensor(tag + "sthab", [128, NG], F32))
                se4b = el(nc.sbuf_tensor(tag + "se4b", [128, 4, NG], F32))
                srs = el(nc.sbuf_tensor(tag + "srs", [128, NG], F32))
                srsi = el(nc.sbuf_tensor(tag + "srsi", [128, NG], F32))
                scnt = el(nc.sbuf_tensor(tag + "scnt", [128, 8], F32))

                dyn = slog9[:, :, 0:8]
                B.v(lambda v: v.tensor_copy(smsk[:], dyn))
                B.vmemset(sno[:], 1.0)
                B.v(lambda v: v.tensor_reduce(sthr[:, 0, :], smsk[:], AX.X,
                                              ALU.max))
                tsub = B.v(lambda v: v.tensor_tensor(
                    st_a[:], dyn,
                    sthr[:, 0, :].unsqueeze(2).broadcast_to([128, NG, 8]),
                    ALU.subtract))
                te1 = B.a(lambda a: a.activation(se1[:], st_a[:], AF.Exp),
                          wait=(tsub,))
                B.v(lambda v: v.tensor_reduce(sZ8[:], se1[:], AX.X,
                                              ALU.add), wait=(te1,))
                B.v(lambda v: v.tensor_single_scalar(sabs[:], dyn, -1.0,
                                                     ALU.mult))
                B.v(lambda v: v.tensor_tensor(sabs[:], sabs[:], dyn, ALU.max))
                for r in range(4):
                    thr_bc = sthr[:, r, :].unsqueeze(2).broadcast_to(
                        [128, NG, 8])
                    B.v(lambda v, thr_bc=thr_bc: v.tensor_tensor(
                        st_a[:], smsk[:], thr_bc, ALU.is_equal))
                    B.v(lambda v: v.tensor_tensor(
                        st_a[:], st_a[:],
                        s8e[:].unsqueeze(1).broadcast_to([128, NG, 8]),
                        ALU.mult))
                    B.v(lambda v: v.tensor_reduce(samx[:], st_a[:], AX.X,
                                                  ALU.max))
                    B.v(lambda v, r=r: v.tensor_tensor(
                        scf[:, :, r, :], st_a[:],
                        samx[:].unsqueeze(2).broadcast_to([128, NG, 8]),
                        ALU.is_equal))
                    B.v(lambda v, r=r: v.tensor_single_scalar(
                        sthn[:], sthr[:, r, :], -1.0, ALU.mult))
                    B.v(lambda v, r=r: v.tensor_tensor(
                        sthab[:], sthn[:], sthr[:, r, :], ALU.max))
                    B.v(lambda v: v.tensor_tensor(
                        st_a[:], sabs[:],
                        sthab[:].unsqueeze(2).broadcast_to([128, NG, 8]),
                        ALU.max))
                    B.v(lambda v: v.tensor_single_scalar(st_a[:], st_a[:], 0.2,
                                                         ALU.mult))
                    B.v(lambda v, thr_bc=thr_bc: v.tensor_tensor(
                        st_b[:], thr_bc, dyn, ALU.subtract))
                    B.v(lambda v: v.tensor_tensor(st_b[:], st_b[:], st_a[:],
                                                  ALU.is_le))
                    B.v(lambda v: v.tensor_tensor(st_b[:], st_b[:], sno[:],
                                                  ALU.mult))
                    B.v(lambda v: v.tensor_tensor(st_b[:], st_b[:], se1[:],
                                                  ALU.mult))
                    B.v(lambda v, r=r: v.tensor_reduce(sD[:, r, :], st_b[:],
                                                       AX.X, ALU.add))
                    B.v(lambda v, r=r: v.scalar_tensor_tensor(
                        sno[:], scf[:, :, r, :], -1.0, sno[:], ALU.mult,
                        ALU.add))
                    B.v(lambda v, r=r: v.scalar_tensor_tensor(
                        smsk[:], scf[:, :, r, :], -1e30, smsk[:], ALU.mult,
                        ALU.add))
                    if r < 3:
                        B.v(lambda v, r=r: v.tensor_reduce(
                            sthr[:, r + 1, :], smsk[:], AX.X, ALU.max))
                tsub4 = B.v(lambda v: v.tensor_tensor(
                    se4b[:], sthr[:],
                    sthr[:, 0:1, :].broadcast_to([128, 4, NG]), ALU.subtract))
                te4 = B.a(lambda a: a.activation(se4[:], se4b[:], AF.Exp),
                          wait=(tsub4,))
                sc1, sc2, sc3 = sthn, sthab, samx
                B.v(lambda v: v.tensor_copy(sc1[:], se4[:, 0, :]),
                    wait=(te4,))
                B.v(lambda v: v.tensor_tensor(sc2[:], sc1[:], se4[:, 1, :],
                                              ALU.add))
                B.v(lambda v: v.tensor_tensor(sc3[:], sc2[:], se4[:, 2, :],
                                              ALU.add))
                B.v(lambda v: v.tensor_single_scalar(szc[:], sZ8[:], 0.75,
                                                     ALU.mult))
                B.v(lambda v: v.reciprocal(sD[:], sD[:]))
                cums = [None, sc1, sc2, sc3]
                for r in range(4):
                    mr = el(nc.sbuf_tensor(tag + "smr%d" % r, [128, NG], F32))
                    B.v(lambda v, r=r, mr=mr: v.tensor_tensor(
                        mr[:], se4[:, r, :], sD[:, r, :], ALU.mult))
                    if r > 0:
                        vr = el(nc.sbuf_tensor(tag + "svr%d" % r, [128, NG],
                                               F32))
                        mr2 = el(nc.sbuf_tensor(tag + "smq%d" % r, [128, NG],
                                                F32))
                        B.v(lambda v, r=r, vr=vr: v.tensor_tensor(
                            vr[:], cums[r][:], szc[:], ALU.is_lt))
                        B.v(lambda v, mr=mr, vr=vr, mr2=mr2: v.tensor_tensor(
                            mr2[:], mr[:], vr[:], ALU.mult))
                        mr = mr2
                    B.v(lambda v, r=r, mr=mr: v.tensor_copy(
                        smv[:, r, :], mr[:]))
                    if r == 0:
                        B.v(lambda v, r=r, mr=mr: v.tensor_tensor(
                            srwA[:], scf[:, :, r, :],
                            mr[:].unsqueeze(2).broadcast_to([128, NG, 8]),
                            ALU.mult))
                    else:
                        B.v(lambda v, r=r, mr=mr: v.tensor_tensor(
                            st_a[:], scf[:, :, r, :],
                            mr[:].unsqueeze(2).broadcast_to([128, NG, 8]),
                            ALU.mult))
                        B.v(lambda v: v.tensor_add(srwA[:], srwA[:], st_a[:]))
                tpre = B.v(lambda v: v.tensor_single_scalar(spre[:], srwA[:],
                                                            0.0, ALU.is_gt))
                B.v(lambda v: v.tensor_reduce(srs[:], srwA[:], AX.X,
                                              ALU.add))
                B.v(lambda v: v.reciprocal(srsi[:], srs[:]))
                B.v(lambda v: v.tensor_tensor(
                    srwn[:], srwA[:],
                    srsi[:].unsqueeze(2).broadcast_to([128, NG, 8]), ALU.mult))
                tcn = B.t(lambda t: t.matmul(
                    p8[:, 6, 0:256], ones_f[:],
                    spre[:].rearrange("p g e -> p (g e)"), start=True,
                    stop=True, skip_group_check=True), wait=(tpre,))
                tcv = B.v(lambda v: v.tensor_reduce(
                    scnt[:],
                    p8[:, 6, 0:256].rearrange("p (g e) -> p g e",
                                              e=8).transpose([0, 2, 1]),
                    AX.X, ALU.add), wait=(tcn,))
                B.serial_v = False
                tccin = B.dma("sync", ccin_ap[cc_i], scnt[0:1, :],
                              wait=(tcv,))
                if cc_i == 0 and dbg:
                    wtok = ("vE", B.val["vE"])
                    for nm, src in (("dbg_thr", sthr[:]), ("dbg_D", sD[:]),
                                    ("dbg_e1", se1[:]), ("dbg_cf", scf[:]),
                                    ("dbg_val", sval[:]), ("dbg_smv", smv[:]),
                                    ("dbg_srw", srwA[:]),
                                    ("dbg_se4", se4[:]),
                                    ("dbg_se4b", se4b[:])):
                        if nm in dbg:
                            B.dma("sync", dbg[nm], src, wait=(wtok,))
                B.flush(skip_q=("qG",) if build_wcache else ())
            return slog9, sl8, srwn, spre, wc_tok, tccin

        def emit_gather(tag, slog9, sl8, srwn, spre, cc_i, w_exp, tccin):
            onehot, tri_f, idf, prefm, ones_f = B.kconsts
            B.phase()
            B.serial_v = True
            with ExitStack() as cl:
                el = cl.enter_context
                skept = el(nc.sbuf_tensor(tag + "skpt", [128, NG, 8], F32))
                soff = el(nc.sbuf_tensor(tag + "soff", [128, 2, 8], F32))
                st8 = el(nc.sbuf_tensor(tag + "st8", [128, 2, 8], F32))
                sg1 = el(nc.sbuf_tensor(tag + "sg1", [128, NG, 2], F32))
                sm9 = el(nc.sbuf_tensor(tag + "sm9", [128, NG], F32))
                sm9b = el(nc.sbuf_tensor(tag + "sm9b", [128, NG], F32))
                sSd = el(nc.sbuf_tensor(tag + "sSd", [128, NG], F32))
                sden = el(nc.sbuf_tensor(tag + "sden", [128, NG], F32))
                sdinv = el(nc.sbuf_tensor(tag + "sdiv", [128, NG], F32))
                sfd = el(nc.sbuf_tensor(tag + "sfd", [128, NG], F32))
                sgsh = el(nc.sbuf_tensor(tag + "sgsh", [128, NG], F32))
                sw8 = el(nc.sbuf_tensor(tag + "sw8", [128, NG, 8], F32))
                sz8 = el(nc.sbuf_tensor(tag + "sz8", [128, NG], F32))
                sz8d = el(nc.sbuf_tensor(tag + "sz8d", [128, NG], F32))
                sd9 = el(nc.sbuf_tensor(tag + "sd9", [128, NG, 9], F32))
                sz9 = el(nc.sbuf_tensor(tag + "sz9", [128, NG, 9], F32))
                w12 = el(nc.sbuf_tensor(tag + "w12", [128, NG, 12], F32))
                p8 = el(nc.psum_tensor(tag + "p8g", [128, 8, TT], F32))

                tcc = B.g(lambda g: g.collective_compute(
                    "AllGather", ALU.bypass,
                    replica_groups=[list(range(NCORE))],
                    ins=[ccin_ap[cc_i]], outs=[ccout_ap[cc_i]]),
                    wait=(tccin,))
                tscc = B.dma("sync", scc[:], ccout_ap[cc_i], wait=(tcc,))
                tpp = B.t(lambda t: t.matmul(p8[:, 4, 0:8], prefm[:], scc[:],
                                             start=True, stop=True,
                                             skip_group_check=True),
                          wait=(tscc,))
                tof = B.v(lambda v: v.tensor_copy(soff[:, 0, :], p8[:, 4, 0:8]),
                          wait=(tpp,))
                bank_rel = {}
                for g in range(NG):
                    ba, bb = g % 2, 2 + g % 2
                    cur, nxt = g % 2, 1 - g % 2
                    t1 = B.t(lambda t, g=g, ba=ba: t.matmul(
                        p8[:, ba, 0:8], tri_f[:], spre[:, g, :], start=True,
                        stop=True, skip_group_check=True),
                        wait=(bank_rel.get(ba), tof))
                    t2 = B.t(lambda t, g=g, bb=bb: t.matmul(
                        p8[:, bb, 0:8], ones_f[:], spre[:, g, :], start=True,
                        stop=True, skip_group_check=True),
                        wait=(bank_rel.get(bb),))
                    t3 = B.v(lambda v, ba=ba, cur=cur: v.tensor_tensor(
                        st8[:, 0, :], p8[:, ba, 0:8], soff[:, cur, :],
                        ALU.add), wait=(t1,))
                    bank_rel[ba] = t3
                    B.v(lambda v: v.tensor_single_scalar(
                        st8[:, 1, :], st8[:, 0, :], CAP, ALU.is_le))
                    B.v(lambda v, g=g: v.tensor_tensor(
                        skept[:, g, :], st8[:, 1, :], spre[:, g, :], ALU.mult))
                    t4 = B.v(lambda v, bb=bb, cur=cur, nxt=nxt: v.tensor_tensor(
                        soff[:, nxt, :], soff[:, cur, :], p8[:, bb, 0:8],
                        ALU.add), wait=(t2,))
                    bank_rel[bb] = t4
                B.v(lambda v: v.tensor_reduce(sm9[:], slog9[:, :, 0:8],
                                              AX.X, ALU.max))
                B.v(lambda v: v.tensor_tensor(sm9b[:], sm9[:],
                                              sl8[:], ALU.max))
                tsd8 = B.v(lambda v: v.tensor_tensor(
                    sd9[:, :, 0:8], slog9[:, :, 0:8],
                    sm9b[:].unsqueeze(2).broadcast_to([128, NG, 8]),
                    ALU.subtract))
                tsl8 = B.v(lambda v: v.tensor_tensor(sz8d[:], sl8[:], sm9b[:],
                                                     ALU.subtract))
                tz = B.a(lambda a: a.activation(sz9[:, :, 0:8],
                                                sd9[:, :, 0:8], AF.Exp),
                         wait=(tsd8,))
                tz8 = B.a(lambda a: a.activation(sz8[:], sz8d[:], AF.Exp),
                          wait=(tsl8,))
                B.v(lambda v: v.tensor_tensor(sd9[:, :, 0:8], sz9[:, :, 0:8],
                                              skept[:], ALU.mult), wait=(tz,))
                B.v(lambda v: v.tensor_reduce(sSd[:], sd9[:, :, 0:8],
                                              AX.X, ALU.add))
                B.v(lambda v: v.tensor_tensor(sden[:], sSd[:],
                                              sz8[:], ALU.add), wait=(tz8,))
                B.v(lambda v: v.reciprocal(sdinv[:], sden[:]))
                B.v(lambda v: v.tensor_single_scalar(sfd[:], sSd[:], 0.5,
                                                     ALU.mult))
                B.v(lambda v: v.tensor_tensor(sfd[:], sfd[:], sdinv[:],
                                              ALU.mult))
                B.v(lambda v: v.scalar_tensor_tensor(
                    sgsh[:], sz8[:], 0.5, sdinv[:], ALU.mult,
                    ALU.mult))
                B.v(lambda v: v.tensor_tensor(
                    sw8[:], srwn[:],
                    sfd[:].unsqueeze(2).broadcast_to([128, NG, 8]), ALU.mult))
                B.v(lambda v: v.tensor_tensor(w12[:, :, 0:8], sw8[:],
                                              skept[:], ALU.mult))
                tw12 = B.v(lambda v: v.tensor_copy(
                    w12[:, :, 8:12],
                    sgsh[:].unsqueeze(2).broadcast_to([128, NG, 4])))
                prel = {}
                for g in range(NG):
                    pb = 5 + g % 2
                    tt1 = B.t(lambda t, g=g, pb=pb: t.transpose(
                        p8[0:12, pb, 0:128], w12[:, g, :], idf[:]),
                        wait=(tw12, prel.get(pb)))
                    prel[pb] = B.v(lambda v, g=g, pb=pb: v.tensor_copy(
                        w_exp[:, g * 128:(g + 1) * 128], p8[0:12, pb, 0:128]),
                        wait=(tt1,))
                B.serial_v = False
                if cc_i == 0 and dbg:
                    wtok = ("vE", B.val["vE"])
                    for nm, src in (("dbg_wexp", w_exp[:]),
                                    ("dbg_log", slog9[:]),
                                    ("dbg_rwn", srwn[:]),
                                    ("dbg_pre", spre[:]),
                                    ("dbg_kept", skept[:])):
                        if nm in dbg:
                            B.dma("sync", dbg[nm], src, wait=(wtok,))
                B.flush()

        def emit_ffn(tag, src_res_ap, dst_ap, wexp_tok, w_exp, wc_tok=None):
            onehot, tri_f, idf, prefm, ones_f = B.kconsts
            if wc_tok is None:
                wc_tok = {}
            B.phase()
            with ExitStack() as cl:
                el = cl.enter_context
                sxln = el(nc.sbuf_tensor(tag + "xln", [128, 2, KC, TT], BF16))
                sxres = el(nc.sbuf_tensor(tag + "res", [128, KC, TT], F32))
                sxout = el(nc.sbuf_tensor(tag + "out", [128, KC, TT], F32))
                swg = el(nc.sbuf_tensor(tag + "wg", [128, 2, KC, 512], BF16))
                swu = el(nc.sbuf_tensor(tag + "wu", [128, 2, KC, 512], BF16))
                swd = el(nc.sbuf_tensor(tag + "wd", [128, 2, NIC, C], BF16))
                ssil = el(nc.sbuf_tensor(tag + "sil", [128, 3, TT], F32))
                sst = el(nc.sbuf_tensor(tag + "sst", [128, 3, TT], F32))
                sa = el(nc.sbuf_tensor(tag + "sa", [128, NU, NIC, TT], BF16))
                p8 = el(nc.psum_tensor(tag + "p8f", [128, 8, TT], F32))

                rel_ln = [None, None]
                rel_res = [None]
                rel_out = [None]
                rel_wg = [None, None]
                rel_wu = [None, None]
                rel_wd = [None, None]
                bank_rel = {b: None for b in range(8)}
                a_rel = None
                for tile in range(NTILE):
                    sq, j = divmod(tile, 4)
                    bf = tile % 2
                    tln = B.dma("sync", sxln[:, bf, :, :],
                                fm_tile(xln_ap, sq, j), wait=(rel_ln[bf],))
                    tres = B.dma("sync", sxres[:],
                                 fm_tile(src_res_ap, sq, j),
                                 wait=(rel_res[0],))
                    a_tok = {}
                    tg = tu = None
                    for e in range(NU):
                        ub = e % 2
                        twg = B.dma("gpsimd", swg[:, ub, :, :],
                                    cwg_ap[e].rearrange("(kc p) i -> p kc i",
                                                        p=128),
                                    wait=(rel_wg[ub],
                                          wc_tok.get((True, False, e))))
                        twu = B.dma("gpsimd", swu[:, ub, :, :],
                                    cwu_ap[e].rearrange("(kc p) i -> p kc i",
                                                        p=128),
                                    wait=(rel_wu[ub],
                                          wc_tok.get((False, True, e))))
                        wb = 6 + e % 2
                        tw = B.t(lambda t, e=e, wb=wb, tile=tile: t.matmul(
                            p8[:, wb, :], onehot[:, e, :],
                            w_exp[:, tile * TT:(tile + 1) * TT], start=True,
                            stop=True, skip_group_check=True),
                            wait=(wexp_tok, bank_rel[wb]))
                        tpw = None
                        for ic in range(NIC):
                            pb = (4 * e + ic) % 3
                            tg = B.mm_group(
                                lambda pb=pb: p8[:, pb, :],
                                [(swg[:, ub, kc, ic * 128:(ic + 1) * 128],
                                  sxln[:, bf, kc, :]) for kc in range(KC)],
                                wait_first=(twg, tln, bank_rel[pb]))
                            tu = B.mm_group(
                                lambda pb=pb: p8[:, 3 + pb, :],
                                [(swu[:, ub, kc, ic * 128:(ic + 1) * 128],
                                  sxln[:, bf, kc, :]) for kc in range(KC)],
                                wait_first=(twu, bank_rel[3 + pb]))
                            ta = B.a(lambda a, pb=pb: a.activation(
                                ssil[:, pb, :], p8[:, pb, :], AF.Silu),
                                wait=(tg,))
                            bank_rel[pb] = ta
                            t1 = B.v(lambda v, pb=pb: v.tensor_tensor(
                                sst[:, pb, :], ssil[:, pb, :],
                                p8[:, 3 + pb, :], ALU.mult), wait=(ta, tu))
                            bank_rel[3 + pb] = t1
                            t2 = B.v(lambda v, e=e, ic=ic, pb=pb, wb=wb:
                                     v.tensor_tensor(
                                         sa[:, e, ic, :], sst[:, pb, :],
                                         p8[:, wb, :], ALU.mult),
                                     wait=(t1, tw, a_rel))
                            tpw = t2
                            a_tok[(e, ic)] = t2
                        bank_rel[wb] = tpw
                        rel_wg[ub] = tg
                        rel_wu[ub] = tu
                    rel_ln[bf] = tu
                    down_stop = {}
                    tl = None
                    for e in range(NU):
                        db = e % 2
                        twd = B.dma("gpsimd", swd[:, db, :, :],
                                    cwd_ap[e].rearrange("(ic p) h -> p ic h",
                                                        p=128),
                                    wait=(rel_wd[db],
                                          wc_tok.get((False, False, e))))
                        for hc in range(8):
                            for ic in range(NIC):
                                w = [twd, a_tok[(e, ic)]]
                                if e == 0 and ic == 0:
                                    w.append(bank_rel[hc])
                                sp = (e == NU - 1 and ic == NIC - 1)
                                last_of_unit = (hc == 7 and ic == NIC - 1)
                                tl = B.op(
                                    "tensor",
                                    lambda t, hc=hc, ic=ic, db=db,
                                    st=(e == 0 and ic == 0), sp=sp, e=e:
                                    t.matmul(
                                        p8[:, hc, :],
                                        swd[:, db, ic, hc * 128:(hc + 1) * 128],
                                        sa[:, e, ic, :], start=st, stop=sp,
                                        skip_group_check=True),
                                    wait=tuple(w),
                                    inc="tE" if (sp or last_of_unit) else None)
                                if sp:
                                    down_stop[hc] = tl
                                if last_of_unit:
                                    rel_wd[db] = tl
                    a_rel = down_stop[7]
                    evs = []
                    for hc in range(8):
                        tev = B.v(lambda v, hc=hc: v.tensor_tensor(
                            sxout[:, hc, :], p8[:, hc, :],
                            sxres[:, hc, :], ALU.add),
                            wait=(down_stop[hc], tres, rel_out[0]))
                        bank_rel[hc] = tev
                        evs.append(tev)
                    rel_res[0] = evs[-1]
                    rel_out[0] = B.dma("sync", fm_tile(dst_ap, sq, j),
                                       sxout[:], wait=tuple(evs))
                    if tile == 0 and dbg and tag == "f1":
                        if "dbg_a" in dbg:
                            B.dma("sync", dbg["dbg_a"], sa[:],
                                  wait=(a_rel,))
                        if "dbg_x1t" in dbg:
                            B.dma("sync", dbg["dbg_x1t"], sxout[:],
                                  wait=tuple(evs))
                B.flush()

        def emit_attn():
            B.phase()
            with ExitStack() as cl:
                el = cl.enter_context
                sx1t = el(nc.sbuf_tensor("a_x1", [128, 2, KC, TT], F32))
                sxl2 = el(nc.sbuf_tensor("a_ln", [128, 2, KC, TT], BF16))
                sstat = el(nc.sbuf_tensor("a_st", [128, 2, TT], F32))
                std_ = el(nc.sbuf_tensor("a_sd", [128, TT], F32))
                sK = el(nc.sbuf_tensor("a_K", [128, KC, S], BF16))
                sQ = el(nc.sbuf_tensor("a_Q", [128, KC, S], BF16))
                sV = el(nc.sbuf_tensor("a_V", [128, S // 128, C], BF16))
                swv = el(nc.sbuf_tensor("a_wv", [128, KC, TT], BF16))
                swx = el(nc.sbuf_tensor("a_wx", [128, 2, KC, 128], BF16))
                sPT = el(nc.sbuf_tensor("a_PT", [128, 8, TT], BF16))
                sO = el(nc.sbuf_tensor("a_O", [128, KC, TT], BF16))
                srinv = el(nc.sbuf_tensor("a_ri", [128, TT], F32))
                sln = el(nc.sbuf_tensor("a_sln", [128, TT], F32))
                sx2t = el(nc.sbuf_tensor("a_x2", [128, KC, TT], F32))
                p8 = el(nc.psum_tensor("a_p8", [128, 8, TT], F32))

                bank_rel = {b: None for b in range(8)}
                rel_x1 = [None, None]
                rel_ln2 = [None, None]
                t6_prev = [None]
                pending = [None]
                sqv_rel = [None]
                relw = [None, None]
                o_rel = [None]
                rinv_rel = [None]
                x2_rel = [None]
                kq_tok = {}
                v_tok = {}
                exp_toks = {}

                slot_state = {"i": 0}
                NSLOT = 8
                slot_rel = [None] * NSLOT

                def make_head(job, prow, b0, av, sm):
                    g, h, n = job
                    base = 64 * (h % 2)
                    oc = h // 2
                    pt = h % 2
                    slots = []
                    ex_toks = []
                    sum_tok = [None]

                    def s_chunk(c):
                        pb = b0 + (c % 2)
                        diag = c >= 4 * g
                        B.op("tensor", lambda t, c=c, pb=pb: t.matmul(
                            p8[:, pb, :],
                            sK[base:base + 64, oc, c * 128:(c + 1) * 128],
                            sQ[base:base + 64, oc, g * TT:(g + 1) * TT],
                            start=True, stop=not diag,
                            skip_group_check=True),
                            wait=(kq_tok[(1, oc, c // 4)],
                                  kq_tok[(0, oc, g)], bank_rel[pb]),
                            inc=None if diag else "tE")
                        if diag:
                            return B.t(lambda t, c=c, pb=pb: t.matmul(
                                p8[:, pb, :], idb[:],
                                cmneg[:, c - 4 * g, :],
                                start=False, stop=True,
                                skip_group_check=True))
                        return ("tE", B.val["tE"])

                    def s_pair(k):
                        t0 = s_chunk(2 * k)
                        t1 = s_chunk(2 * k + 1)
                        sl = slot_state["i"] % NSLOT
                        slot_state["i"] += 2
                        slots.append(sl)
                        return (t0, t1, sl)

                    def set_exp(tex):
                        bank_rel[b0] = tex
                        bank_rel[b0 + 1] = tex
                        ex_toks.append(tex)

                    def tails_pair(k):
                        sl = slots[k]
                        tex = ex_toks[k]
                        last = (2 * k + 1 == n - 1)
                        for ci in range(2):
                            c = 2 * k + ci
                            w = [tex]
                            if c == 0:
                                w.append(bank_rel[sm])
                            tsm = B.op("tensor", lambda t, sl=sl, ci=ci:
                                       t.matmul(
                                           p8[prow:prow + 64, sm, :],
                                           ones_bf[:, 0:64],
                                           sPT[:, sl + ci, :],
                                           start=(2 * k + ci == 0),
                                           stop=(2 * k + ci == n - 1),
                                           skip_group_check=True),
                                       wait=tuple(w),
                                       inc="tE" if 2 * k + ci == n - 1
                                       else None)
                            if 2 * k + ci == n - 1:
                                sum_tok[0] = tsm
                        tav = None
                        for ci in range(2):
                            c = 2 * k + ci
                            w = [tex, v_tok[c]]
                            if c == 0:
                                w.append(bank_rel[av])
                            tav = B.op("tensor", lambda t, sl=sl, ci=ci, c=c:
                                       t.matmul(
                                           p8[prow:prow + 64, av, :],
                                           sV[:, c, 64 * h:64 * h + 64],
                                           sPT[:, sl + ci, :],
                                           start=(c == 0), stop=(c == n - 1),
                                           skip_group_check=True),
                                       wait=tuple(w), inc="tE")
                            slot_rel[sl + ci] = tav
                        return tav

                    def finish(tav, trinv):
                        tev = B.v(lambda v: v.tensor_tensor(
                            sO[base:base + 64, oc, :],
                            p8[prow:prow + 64, av, :],
                            srinv[prow:prow + 64, :],
                            ALU.mult),
                            wait=(tav, trinv, o_rel[0]))
                        bank_rel[av] = tev
                        return tev

                    return (n, s_pair, tails_pair, finish, sum_tok,
                            set_exp, slots)

                def emit_pair(jobA, jobB, pp):
                    av = 4 + (pp % 2)
                    sm = 4 + ((pp + 1) % 2)
                    (nA, sA, tA, fA, stA, seA,
                     slotsA) = make_head(jobA, 0, 0, av, sm)
                    (nB, sB, tB, fB, stB, seB,
                     slotsB) = make_head(jobB, 64, 2, av, sm)
                    npair = nA // 2

                    def exp_k(k):
                        tA0, tA1, slA = sA(k)
                        tB0, tB1, slB = sB(k)
                        assert slB == slA + 2
                        tex = B.a(lambda a, slA=slA: a.activation(
                            sPT[:, slA:slA + 4, :], p8[:, 0:4, :],
                            AF.Exp, bias=0.0, scale=ATTN_SCALE),
                            wait=(tA0, tA1, tB0, tB1,
                                  slot_rel[slA], slot_rel[slA + 1],
                                  slot_rel[slA + 2], slot_rel[slA + 3]))
                        seA(tex)
                        seB(tex)

                    exp_k(0)
                    if pending[0] is not None:
                        pending[0]()
                        pending[0] = None
                    for k in range(1, npair):
                        exp_k(k)
                        tA(k - 1)
                        tB(k - 1)
                    tavA = tA(npair - 1)
                    tavB = tB(npair - 1)

                    def finish_pair(done_cb):
                        def run():
                            rw = [stA[0], stB[0]]
                            if rinv_rel[0] is not None:
                                rw.extend(rinv_rel[0])
                            tl1 = B.a(lambda a: a.activation(
                                sln[:], p8[:, sm, :], AF.Ln, bias=0.0,
                                scale=1.0), wait=tuple(rw))
                            bank_rel[sm] = tl1
                            tl2 = B.a(lambda a: a.activation(
                                srinv[:], sln[:], AF.Exp, bias=0.0,
                                scale=-1.0))
                            tevA = fA(tavA, tl2)
                            tevB = fB(tavB, tl2)
                            rinv_rel[0] = (tevA, tevB)
                            done_cb(tevA, tevB)
                        return run
                    return finish_pair

                def emit_wo(sq, g, otoks):
                    tr = B.dma("sync", sx2t[:], fm_tile(x1_ap, sq, g),
                               wait=(x2_rel[0],))
                    evs = []
                    for oc in range(KC):
                        sb = oc % 2
                        twl = B.dma("gpsimd", swx[:, sb, :, :],
                                    cwq_ap[3].rearrange(
                                        "(kc p) o -> p kc o",
                                        p=128)[:, :, oc * 128:(oc + 1) * 128],
                                    wait=(relw[sb],))
                        pb = 6 + oc % 2
                        tpo = B.mm_group(
                            lambda pb=pb: p8[:, pb, :],
                            [(swx[:, sb, kc, :], sO[:, kc, :])
                             for kc in range(KC)],
                            wait_first=tuple([twl, bank_rel[pb]] + otoks))
                        relw[sb] = tpo
                        tev = B.v(lambda v, oc=oc, pb=pb: v.tensor_tensor(
                            sx2t[:, oc, :], sx2t[:, oc, :], p8[:, pb, :],
                            ALU.add), wait=(tpo, tr))
                        bank_rel[pb] = tev
                        evs.append(tev)
                    o_rel[0] = evs[-1]
                    x2_rel[0] = B.dma("sync", fm_tile(x2_ap, sq, g), sx2t[:],
                                      wait=tuple(evs))

                wv_rel = [None]
                for sq in range(BSEQ):
                    kq_tok.clear()
                    v_tok.clear()
                    def emit_ln(j):
                        bf = j % 2
                        tx = B.dma("sync", sx1t[:, bf, :, :],
                                   fm_tile(x1_ap, sq, j),
                                   wait=(rel_x1[bf],))
                        tb16 = B.v(lambda v, bf=bf: v.tensor_copy(
                            sxl2[:, bf, :, :], sx1t[:, bf, :, :]),
                            wait=(tx, rel_ln2[bf]))
                        sqv = sPT
                        tq2 = B.v(lambda v, bf=bf: v.tensor_tensor(
                            sPT[:, 0:KC, :], sxl2[:, bf, :, :],
                            sxl2[:, bf, :, :], ALU.mult),
                            wait=tuple([sqv_rel[0]] + list(slot_rel)))
                        return lambda: emit_ln_post(bf, tb16, tq2)

                    def emit_ln_post(bf, tb16, tq2):
                        tm = B.mm_group(
                            lambda: p8[:, 6, :],
                            [(ones_bf[:], sxl2[:, bf, kc, :])
                             for kc in range(KC)],
                            wait_first=(tb16, bank_rel[6]))
                        t1 = B.v(lambda v: v.tensor_single_scalar(
                            sstat[:, 0, :], p8[:, 6, :], 1.0 / C,
                            ALU.mult), wait=(tm,))
                        bank_rel[6] = t1
                        tmq = B.mm_group(
                            lambda: p8[:, 7, :],
                            [(ones_bf[:], sPT[:, kc, :])
                             for kc in range(KC)],
                            wait_first=(tq2, bank_rel[7]))
                        sqv_rel[0] = tmq
                        for _sl in range(NSLOT):
                            slot_rel[_sl] = tmq
                        t2 = B.v(lambda v: v.tensor_single_scalar(
                            sstat[:, 1, :], p8[:, 7, :], 1.0 / C,
                            ALU.mult), wait=(tmq,))
                        bank_rel[7] = t2
                        B.v(lambda v: v.scalar_tensor_tensor(
                            std_[:], sstat[:, 0, :], -1.0,
                            sstat[:, 0, :], ALU.mult, ALU.mult),
                            wait=(t6_prev[0],))
                        t4 = B.v(lambda v: v.tensor_add(
                            sstat[:, 1, :], sstat[:, 1, :], std_[:]))
                        B.a(lambda a: a.activation(
                            std_[:], sstat[:, 1, :], AF.Ln,
                            bias=seps[:], scale=1.0), wait=(t4,))
                        t6 = B.a(lambda a: a.activation(
                            sstat[:, 1, :], std_[:], AF.Exp,
                            bias=0.0, scale=-0.5))
                        t6_prev[0] = t6
                        tn = None
                        for kc in range(KC):
                            B.v(lambda v, bf=bf, kc=kc: v.tensor_sub(
                                std_[:], sx1t[:, bf, kc, :], sstat[:, 0, :]),
                                wait=(t6,) if kc == 0 else ())
                            tn = B.v(lambda v, bf=bf, kc=kc: v.tensor_tensor(
                                sxl2[:, bf, kc, :], std_[:], sstat[:, 1, :],
                                ALU.mult))
                        rel_x1[bf] = tn
                        return tn

                    def emit_qkv(j, tn, mid_post=None):
                        bf = j % 2
                        tn_new = None
                        for wi, dst in ((1, sK), (0, sQ)):
                            if wi == 0 and mid_post is not None:
                                tn_new = mid_post()
                            for oc in range(KC):
                                sb = oc % 2
                                twl = B.dma(
                                    "gpsimd", swx[:, sb, :, :],
                                    cwq_ap[wi].rearrange(
                                        "(kc p) o -> p kc o",
                                        p=128)[:, :, oc * 128:(oc + 1) * 128],
                                    wait=(relw[sb],))
                                pb = 4 if wi == 1 else 5
                                tpk = B.mm_group(
                                    lambda pb=pb: p8[:, pb, :],
                                    [(swx[:, sb, kc, :], sxl2[:, bf, kc, :])
                                     for kc in range(KC)],
                                    wait_first=(twl, tn, bank_rel[pb]))
                                relw[sb] = tpk
                                rel_ln2[bf] = tpk
                                tcp = B.v(lambda v, dst=dst, oc=oc, j=j, pb=pb:
                                          v.tensor_copy(
                                              dst[:, oc,
                                                  j * TT:(j + 1) * TT],
                                              p8[:, pb, :]), wait=(tpk,))
                                bank_rel[pb] = tcp
                                kq_tok[(wi, oc, j)] = tcp
                        for og in range(2):
                            twv = B.dma(
                                "gpsimd", swv[:],
                                cwq_ap[2].rearrange(
                                    "(kc p) o -> p kc o",
                                    p=128)[:, :, og * TT:(og + 1) * TT],
                                wait=(wv_rel[0],))
                            for tc in range(4):
                                tch = j * 4 + tc
                                pb = tc % 2
                                tpv = B.mm_group(
                                    lambda pb=pb: p8[:, pb, :],
                                    [(sxl2[:, bf, kc, tc * 128:(tc + 1) * 128],
                                      swv[:, kc, :])
                                     for kc in range(KC)],
                                    wait_first=(twv, tn, bank_rel[pb]))
                                rel_ln2[bf] = tpv
                                wv_rel[0] = tpv
                                tcv = B.v(lambda v, tch=tch, og=og, pb=pb:
                                          v.tensor_copy(
                                              sV[:, tch,
                                                 og * TT:(og + 1) * TT],
                                              p8[:, pb, :]), wait=(tpv,))
                                bank_rel[pb] = tcv
                                if og == 1:
                                    v_tok[tch] = tcv
                                else:
                                    v_tok[tch] = tcv
                        return tn_new

                    tn_prev = emit_ln(0)()
                    for j in range(1, 4):
                        post_new = emit_ln(j)
                        tn_prev = emit_qkv(j - 1, tn_prev, post_new)
                    emit_qkv(3, tn_prev)

                    o_done = {}

                    def mk_done(g):
                        def cb(ta, tb):
                            o_done.setdefault(g, []).extend([ta, tb])
                            if len(o_done[g]) == NH:
                                emit_wo(sq, g, o_done[g])
                        return cb

                    pp = 0
                    for g in range(4):
                        for h in range(0, NH, 2):
                            fin = emit_pair((g, h, 4 * g + 4),
                                            (g, h + 1, 4 * g + 4), pp)
                            pending[0] = fin(mk_done(g))
                            pp += 1
                    if pending[0] is not None:
                        pending[0]()
                        pending[0] = None
                B.flush()

        with ExitStack() as span1:
            w_exp = span1.enter_context(
                nc.sbuf_tensor("w_exp1", [12, TPC], BF16))
            (slog9, sl8, srwn, spre, wc_tok,
             tccin) = emit_route(span1, "r1", x_ap, True, 0)
            if _STOP != "P1":
                emit_gather("g1", slog9, sl8, srwn, spre, 0, w_exp, tccin)
                wexp_tok = ("vE", B.val["vE"])
                if _STOP != "P2":
                    emit_ffn("f1", x_ap, x1_ap, wexp_tok, w_exp, wc_tok)
        if _STOP is None:
            emit_attn()
            with ExitStack() as span2:
                w_exp = span2.enter_context(
                    nc.sbuf_tensor("w_exp2", [12, TPC], BF16))
                (slog9, sl8, srwn, spre, _wct,
                 tccin) = emit_route(span2, "r2", x2_ap, False, 1)
                emit_gather("g2", slog9, sl8, srwn, spre, 1, w_exp, tccin)
                wexp_tok = ("vE", B.val["vE"])
                emit_ffn("f2", x2_ap, out_ap, wexp_tok, w_exp)

        B.phase()
        for eng, q in B.queue_sem.items():
            B.ops[eng].append((((q, B.val[q]),), None, None, 0))
        B.flush()

    return nc


def _prep(inputs):
    f = np.float32

    def c(a):
        return np.ascontiguousarray(a, dtype=f)

    wqkvo = c(np.stack([inputs["L0_attn_Wq"].T, inputs["L0_attn_Wk"].T,
                        inputs["L0_attn_Wv"].T, inputs["L0_attn_Wo"].T]))
    dg = np.asarray(inputs["L0_ffn_dyn_gate_w"]).transpose(0, 2, 1)
    du = np.asarray(inputs["L0_ffn_dyn_up_w"]).transpose(0, 2, 1)
    dd = np.asarray(inputs["L0_ffn_dyn_down_w"]).transpose(0, 2, 1)
    sg_ = np.asarray(inputs["L0_ffn_sh_gate_w"])[0].T.reshape(
        1024, 4, 512).transpose(1, 0, 2)
    su_ = np.asarray(inputs["L0_ffn_sh_up_w"])[0].T.reshape(
        1024, 4, 512).transpose(1, 0, 2)
    sd_ = np.asarray(inputs["L0_ffn_sh_down_w"])[0].T.reshape(4, 512, 1024)
    wg = c(np.concatenate([dg, sg_], axis=0))
    wu = c(np.concatenate([du, su_], axis=0))
    wd = c(np.concatenate([dd, sd_], axis=0))
    gwt = c(inputs["L0_ffn_gate_w"].T)
    return wqkvo, wg, wu, wd, gwt


def kernel(**inputs):
    x = np.asarray(inputs["L0_attn_x"], dtype=np.float32)
    xT = np.ascontiguousarray(x.transpose(0, 2, 1))
    wqkvo, wg, wu, wd, gwt = _prep(inputs)
    nc = build_program()
    import ml_dtypes
    kco = np.zeros((12, NU, 128), dtype=ml_dtypes.bfloat16)
    for e in range(NU):
        kco[e, e, :] = 1.0
    iot = np.arange(128, dtype=np.float32)
    ktri = (iot[None, :] >= iot[:, None]).astype(np.float32)
    kid = np.eye(128, dtype=np.float32)
    in_maps = []
    for c in range(NCORE):
        in_maps.append({
            "x": np.ascontiguousarray(xT[2 * c:2 * c + 2]),
            "wqkvo": wqkvo, "wg": wg, "wu": wu, "wd": wd, "gwt": gwt,
            "cid": np.full((128, 1), float(c), dtype=np.float32),
            "kco": kco, "ktri": ktri, "kid": kid,
            "kpre": np.broadcast_to(
                (np.arange(8) < c)[:, None], (8, 128)
            ).astype(np.float32).copy(),
        })
    res = run_bass_kernel_spmd(nc, in_maps, core_ids=list(range(NCORE)))
    out = np.empty((16, S, C), dtype=np.float32)
    for c in range(NCORE):
        o = np.asarray(res.results[c]["out"], dtype=np.float32)
        out[2 * c:2 * c + 2] = o.transpose(0, 2, 1)
    globals()["_last_results"] = res
    return out


# version 22 (final)
# speedup vs baseline: 1.0738x; 1.0037x over previous
import numpy as np
from contextlib import ExitStack

import concourse.bass as bass
import concourse.mybir as mybir
from concourse.bass_utils import run_bass_kernel_spmd

F32 = mybir.dt.float32
BF16 = mybir.dt.bfloat16
AF = mybir.ActivationFunctionType
ALU = mybir.AluOpType
AX = mybir.AxisListType

NCORE = 8
BSEQ = 2
S = 2048
C = 1024
KC = 8
NH = 16
HD = 64
NU = 12
NIC = 4
TT = 512
NTILE = 8
NG = 32
TPC = BSEQ * S
CAP = 5120.0
LNEPS = 1e-5
ATTN_SCALE = 0.125

_DEBUG = set()
_STOP = None

ENGS = ("sync", "gpsimd", "tensor", "vector", "scalar")


class Bld:

    def __init__(self, nc):
        self.nc = nc
        self.sems = {}
        self.val = {}
        self.waited = {e: {} for e in ENGS}
        self.ops = None
        self.queue_sem = {"sync": "qS", "gpsimd": "qG", "scalar": "qA"}

    def add_sem(self, name, handle):
        self.sems[name] = handle
        self.val[name] = 0

    def phase(self):
        self.ops = {e: [] for e in ENGS}

    def op(self, eng, fn, wait=(), inc=None, n=1):
        w = tuple(t for t in wait if t is not None)
        tok = None
        if inc is not None:
            self.val[inc] += n
            tok = (inc, self.val[inc])
        self.ops[eng].append((w, fn, inc, n))
        return tok

    def dma(self, eng, out, in_, wait=()):
        q = self.queue_sem[eng]
        return self.op(eng, lambda e, o=out, i=in_: e.dma_start(out=o, in_=i),
                       wait, inc=q, n=16)

    def t(self, fn, wait=()):
        return self.op("tensor", fn, wait, inc="tE")

    serial_v = False

    def v(self, fn, wait=()):
        tok = self.op("vector", fn, wait, inc="vE")
        if self.serial_v:
            self.ops["vector"].append(((tok,), None, None, 0))
        return tok

    def a(self, fn, wait=()):
        return self.op("scalar", fn, wait, inc="aE")

    def g(self, fn, wait=()):
        return self.op("gpsimd", fn, wait, inc="pE")

    def vmemset(self, ap, val):
        tok = self.v(lambda v, a=ap, c=val: v.memset(a, c))
        self.ops["vector"].append(((tok,), None, None, 0))
        return tok

    def mm_group(self, bank_fn, ops, wait_first=()):
        n = len(ops)
        tok = None
        for i, (lhsT, rhs) in enumerate(ops):
            w = wait_first if i == 0 else ()
            tok = self.op(
                "tensor",
                lambda t, bf=bank_fn, l=lhsT, r=rhs, st=(i == 0),
                sp=(i == n - 1): t.matmul(bf(), l, r, start=st, stop=sp,
                                          skip_group_check=True),
                w, inc="tE" if i == n - 1 else None)
        return tok

    def flush(self, skip_q=()):
        nc = self.nc
        for eng, q in self.queue_sem.items():
            if q in skip_q:
                continue
            if self.val.get(q, 0) > self.waited[eng].get(q, 0) and \
                    any(i == q for (_, _, i, _) in self.ops[eng]):
                self.ops[eng].append((((q, self.val[q]),), None, None, 0))
        with nc.Block() as block:
            for eng in ENGS:
                lst = self.ops[eng]
                if not lst:
                    continue
                getattr(block, eng)(self._body(eng, lst))
        self.ops = None

    def _body(self, eng, lst):
        def body(e):
            for (waits, fn, inc, n) in lst:
                for (sname, v) in waits:
                    if self.waited[eng].get(sname, 0) < v:
                        e.wait_ge(self.sems[sname], v)
                        self.waited[eng][sname] = v
                if fn is None:
                    continue
                inst = fn(e)
                if inc is not None:
                    inst.then_inc(self.sems[inc], n)
        return body


def fm_tile(ap3, s, j, w=TT):
    return ap3[s].rearrange("(kc p) t -> p kc t", p=128)[:, :, j * w:(j + 1) * w]


def build_program():
    nc = bass.Bass(target_bir_lowering=False)

    def dram(name, shape, dtype, kind=None):
        if kind is None and name in _DEBUG:
            kind = "ExternalOutput"
        if kind is None:
            return nc.dram_tensor(name, shape, dtype).ap()
        return nc.dram_tensor(name, shape, dtype, kind=kind).ap()

    x_ap = dram("x", [BSEQ, C, S], F32, "ExternalInput")
    wg_ap = dram("wg", [NU, C, 512], F32, "ExternalInput")
    wu_ap = dram("wu", [NU, C, 512], F32, "ExternalInput")
    wd_ap = dram("wd", [NU, 512, C], F32, "ExternalInput")
    wq4_ap = dram("wqkvo", [4, C, C], F32, "ExternalInput")
    gwt_ap = dram("gwt", [C, 9], F32, "ExternalInput")
    cid_ap = dram("cid", [128, 1], F32, "ExternalInput")
    kco_ap = dram("kco", [12, NU, 128], BF16, "ExternalInput")
    ktri_ap = dram("ktri", [128, 128], F32, "ExternalInput")
    kid_ap = dram("kid", [128, 128], F32, "ExternalInput")
    kpre_ap = dram("kpre", [8, 128], F32, "ExternalInput")
    out_ap = dram("out", [BSEQ, C, S], F32, "ExternalOutput")

    x1_ap = dram("x1", [BSEQ, C, S], F32)
    x2_ap = dram("x2", [BSEQ, C, S], F32)
    xln_ap = dram("xln", [BSEQ, C, S], BF16)
    cwg_ap = dram("cwg", [NU, C, 512], BF16)
    cwu_ap = dram("cwu", [NU, C, 512], BF16)
    cwd_ap = dram("cwd", [NU, 512, C], BF16)
    cwq_ap = dram("cwq", [4, C, C], BF16)
    dbg = {}
    for nm, shp, dt in (("dbg_wexp", [12, TPC], BF16),
                        ("dbg_log", [128, NG, 9], F32),
                        ("dbg_rwn", [128, NG, 8], F32),
                        ("dbg_pre", [128, NG, 8], F32),
                        ("dbg_kept", [128, NG, 8], F32),
                        ("dbg_thr", [128, 4, NG], F32),
                        ("dbg_D", [128, 4, NG], F32),
                        ("dbg_e1", [128, NG, 8], F32),
                        ("dbg_cf", [128, NG, 4, 8], F32),
                        ("dbg_val", [128, NG, 4], F32),
                        ("dbg_smv", [128, 4, NG], F32),
                        ("dbg_stc", [128, NG, 2], F32),
                        ("dbg_srw", [128, NG, 8], F32),
                        ("dbg_se4", [128, 4, NG], F32),
                        ("dbg_se4b", [128, 4, NG], F32),
                        ("dbg_a", [128, NU, NIC, TT], BF16),
                        ("dbg_x1t", [128, KC, TT], F32)):
        if nm in _DEBUG:
            dbg[nm] = dram(nm, shp, dt, "ExternalOutput")
    ccin_ap = [dram("ccin%d" % i, [1, 8], F32) for i in range(2)]
    ccout_ap = [dram("ccout%d" % i, [NCORE, 8], F32) for i in range(2)]

    B = Bld(nc)

    with ExitStack() as top:
        et = top.enter_context
        for sname in ("qS", "qG", "qA", "tE", "vE", "aE", "pE"):
            B.add_sem(sname, et(nc.semaphore(sname)))

        ones_bf = et(nc.sbuf_tensor("ones_bf", [128, 128], BF16))
        cmneg = et(nc.sbuf_tensor("cmneg", [128, 4, TT], BF16))
        idb = et(nc.sbuf_tensor("idb", [128, 128], BF16))
        s8e = et(nc.sbuf_tensor("s8e", [128, 8], F32))
        scid = et(nc.sbuf_tensor("scid", [128, 1], F32))
        scc = et(nc.sbuf_tensor("scc", [8, 8], F32))
        seps = et(nc.sbuf_tensor("seps", [128, 1], F32))

        B.phase()
        with ExitStack() as c0:
            e0 = c0.enter_context
            it_p = e0(nc.sbuf_tensor("it_p", [128, TT], F32))
            it_f = e0(nc.sbuf_tensor("it_f", [128, TT], F32))
            sdm = e0(nc.sbuf_tensor("sdm", [128, TT], F32))
            wstg0 = e0(nc.sbuf_tensor("wstg0", [128, 2, KC, 1024], BF16))

            tcid = B.dma("sync", scid[:], cid_ap)
            tip = B.g(lambda g: g.iota(it_p[:], [[0, TT]], channel_multiplier=1,
                                       allow_small_or_imprecise_dtypes=True))
            tif = B.g(lambda g: g.iota(it_f[:], [[1, TT]], channel_multiplier=0,
                                       allow_small_or_imprecise_dtypes=True))
            tie = B.g(lambda g: g.iota(s8e[:], [[1, 8]], channel_multiplier=0,
                                       allow_small_or_imprecise_dtypes=True))
            B.vmemset(seps[:], LNEPS)
            B.vmemset(ones_bf[:], 1.0)
            B.v(lambda v: v.tensor_tensor(idb[:], it_p[:, 0:128],
                                          it_f[:, 0:128], ALU.is_equal),
                wait=(tip, tif))
            tsd = B.v(lambda v: v.tensor_sub(sdm[:], it_f[:], it_p[:]),
                      wait=(tip, tif))
            for d in range(4):
                tlt = B.v(lambda v, d=d: v.tensor_single_scalar(
                    cmneg[:, d, :], sdm[:], float(128 * d), ALU.is_lt),
                    wait=(tsd,))
                B.v(lambda v, d=d: v.tensor_single_scalar(
                    cmneg[:, d, :], cmneg[:, d, :], -800.0, ALU.mult),
                    wait=(tlt,))
            tng = B.v(lambda v: v.tensor_single_scalar(
                s8e[:], s8e[:], -1.0, ALU.mult), wait=(tie,))
            B.v(lambda v: v.tensor_single_scalar(s8e[:], s8e[:], 8.0, ALU.add),
                wait=(tng,))
            relw = [None, None]
            for m in range(4):
                b = m % 2
                tl = B.dma("gpsimd", wstg0[:, b, :, :],
                           wq4_ap[m].rearrange("(kc p) o -> p kc o", p=128),
                           wait=(relw[b],))
                relw[b] = B.dma("gpsimd",
                                cwq_ap[m].rearrange("(kc p) o -> p kc o", p=128),
                                wstg0[:, b, :, :], wait=(tl,))
            B.flush()

        def emit_route(ctx, tag, src_ap, build_wcache, cc_i):
            e1 = ctx.enter_context
            slog9 = e1(nc.sbuf_tensor(tag + "slog9", [128, NG, 9], F32))
            sl8 = e1(nc.sbuf_tensor(tag + "sl8", [128, NG], F32))
            sgw = e1(nc.sbuf_tensor(tag + "sgw", [128, KC, 9], F32))
            srwn = e1(nc.sbuf_tensor(tag + "srwn", [128, NG, 8], F32))
            spre = e1(nc.sbuf_tensor(tag + "spre", [128, NG, 8], F32))
            wstg = e1(nc.sbuf_tensor(tag + "wstg", [128, 2, KC, TT],
                                     BF16)) if build_wcache else None
            onehot = e1(nc.sbuf_tensor(tag + "oh", [12, NU, 128], BF16))
            tri_f = e1(nc.sbuf_tensor(tag + "tri", [128, 128], F32))
            idf = e1(nc.sbuf_tensor(tag + "idf", [128, 128], F32))
            prefm = e1(nc.sbuf_tensor(tag + "pre", [8, 128], F32))
            ones_f = e1(nc.sbuf_tensor(tag + "onef", [128, 128], F32))
            B.phase()
            with ExitStack() as cl:
                el = cl.enter_context
                sxt = el(nc.sbuf_tensor(tag + "sxt", [128, 2, KC, TT], F32))
                sxb = el(nc.sbuf_tensor(tag + "sxb", [128, 2, KC, TT], BF16))
                sxq = el(nc.sbuf_tensor(tag + "sxq", [128, 2, KC, TT], BF16))
                sxln32 = el(nc.sbuf_tensor(tag + "sl32", [128, 2, KC, TT], F32))
                sxlnb = el(nc.sbuf_tensor(tag + "slb", [128, 2, KC, TT], BF16))
                sstat = el(nc.sbuf_tensor(tag + "stat", [128, 2, 4, TT], F32))
                std_ = el(nc.sbuf_tensor(tag + "std", [128, 2, TT], F32))
                p8 = el(nc.psum_tensor(tag + "p8r", [128, 8, TT], F32))

                tgw = B.dma("sync", sgw[:],
                            gwt_ap.rearrange("(kc p) e -> p kc e", p=128))
                B.dma("sync", onehot[:], kco_ap)
                B.dma("sync", tri_f[:], ktri_ap)
                B.dma("sync", idf[:], kid_ap)
                B.dma("sync", prefm[:], kpre_ap)
                B.kconsts = (onehot, tri_f, idf, prefm, ones_f)
                B.vmemset(ones_f[:], 1.0)
                rel_buf = [None, None]
                rel_lnb = [None, None]
                rel_wc = [None, None]
                wc_tok = {}
                btok = {}
                for tile in range(NTILE):
                    sq, j = divmod(tile, 4)
                    bf = tile % 2
                    tx = B.dma("sync", sxt[:, bf, :, :], fm_tile(src_ap, sq, j),
                               wait=(rel_buf[bf],))
                    txb = B.v(lambda v, bf=bf: v.tensor_copy(
                        sxb[:, bf, :, :], sxt[:, bf, :, :]), wait=(tx,))
                    txq = B.v(lambda v, bf=bf: v.tensor_tensor(
                        sxq[:, bf, :, :], sxt[:, bf, :, :], sxt[:, bf, :, :],
                        ALU.mult))
                    tm = B.mm_group(
                        lambda bf=bf: p8[:, 2 * bf, :],
                        [(ones_bf[:], sxb[:, bf, kc, :]) for kc in range(KC)],
                        wait_first=(txb, btok.get((bf, "sA"))))
                    tq = B.mm_group(
                        lambda bf=bf: p8[:, 2 * bf + 1, :],
                        [(ones_bf[:], sxq[:, bf, kc, :]) for kc in range(KC)],
                        wait_first=(txq, btok.get((bf, "sB"))))
                    btok[(bf, "sA")] = B.v(
                        lambda v, bf=bf: v.tensor_single_scalar(
                            sstat[:, bf, 0, :], p8[:, 2 * bf, :], 1.0 / C,
                            ALU.mult), wait=(tm,))
                    btok[(bf, "sB")] = B.v(
                        lambda v, bf=bf: v.tensor_single_scalar(
                            sstat[:, bf, 1, :], p8[:, 2 * bf + 1, :], 1.0 / C,
                            ALU.mult), wait=(tq,))
                    B.v(lambda v, bf=bf: v.scalar_tensor_tensor(
                        sstat[:, bf, 2, :], sstat[:, bf, 0, :], -1.0,
                        sstat[:, bf, 0, :], ALU.mult, ALU.mult))
                    t4 = B.v(lambda v, bf=bf: v.tensor_add(
                        sstat[:, bf, 2, :], sstat[:, bf, 2, :],
                        sstat[:, bf, 1, :]))
                    B.a(lambda a, bf=bf: a.activation(
                        sstat[:, bf, 3, :], sstat[:, bf, 2, :], AF.Ln,
                        bias=seps[:], scale=1.0), wait=(t4,))
                    t6 = B.a(lambda a, bf=bf: a.activation(
                        sstat[:, bf, 2, :], sstat[:, bf, 3, :], AF.Exp,
                        bias=0.0, scale=-0.5))
                    tn = None
                    for kc in range(KC):
                        B.v(lambda v, bf=bf, kc=kc: v.tensor_sub(
                            std_[:, bf, :], sxt[:, bf, kc, :],
                            sstat[:, bf, 0, :]),
                            wait=(t6,) if kc == 0 else ())
                        tn = B.v(lambda v, bf=bf, kc=kc: v.tensor_tensor(
                            sxln32[:, bf, kc, :], std_[:, bf, :],
                            sstat[:, bf, 2, :], ALU.mult))
                    tb = B.v(lambda v, bf=bf: v.tensor_copy(
                        sxlnb[:, bf, :, :], sxln32[:, bf, :, :]),
                        wait=(rel_lnb[bf],))
                    rel_lnb[bf] = B.dma("sync", fm_tile(xln_ap, sq, j),
                                        sxlnb[:, bf, :, :], wait=(tb,))
                    last_pl = None
                    for tc in range(4):
                        gidx = tile * 4 + tc
                        pb = 4 + (gidx % 2)
                        tlg = B.mm_group(
                            lambda pb=pb: p8[:, pb, 0:9],
                            [(sxln32[:, bf, kc, tc * 128:(tc + 1) * 128],
                              sgw[:, kc, :]) for kc in range(KC)],
                            wait_first=(tn, tgw, btok.get(("pl", pb))))
                        B.v(lambda v, gidx=gidx, pb=pb: v.tensor_copy(
                            slog9[:, gidx, :], p8[:, pb, 0:9]), wait=(tlg,))
                        last_pl = B.v(
                            lambda v, gidx=gidx, pb=pb: v.tensor_copy(
                                sl8[:, gidx:gidx + 1], p8[:, pb, 8:9]))
                        btok[("pl", pb)] = last_pl
                    rel_buf[bf] = last_pl

                    if build_wcache and tile < 6:
                        for k in range(2):
                            e = 2 * tile + k
                            wb = (2 * tile + k) % 2
                            for (src, dst, nch, ncol) in (
                                    (wg_ap, cwg_ap, 8, 512),
                                    (wu_ap, cwu_ap, 8, 512),
                                    (wd_ap, cwd_ap, 4, 1024)):
                                stg = wstg[:, wb, :, :].rearrange(
                                    "p a b -> p (a b)").rearrange(
                                    "p (c d) -> p c d", d=ncol)[:, 0:nch, :]
                                tl = B.dma("gpsimd", stg,
                                           src[e].rearrange("(a p) b -> p a b",
                                                            p=128),
                                           wait=(rel_wc[wb],))
                                rel_wc[wb] = B.dma(
                                    "gpsimd",
                                    dst[e].rearrange("(a p) b -> p a b", p=128),
                                    stg, wait=(tl,))
                                wc_tok[(dst is cwg_ap, dst is cwu_ap, e)] = \
                                    rel_wc[wb]

                B.serial_v = True
                smsk = el(nc.sbuf_tensor(tag + "smsk", [128, NG, 8], F32))
                se1 = el(nc.sbuf_tensor(tag + "se1", [128, NG, 8], F32))
                sno = el(nc.sbuf_tensor(tag + "sno", [128, NG, 8], F32))
                scf = el(nc.sbuf_tensor(tag + "scf", [128, NG, 4, 8], F32))
                sthr = el(nc.sbuf_tensor(tag + "sthr", [128, 4, NG], F32))
                sD = el(nc.sbuf_tensor(tag + "sD", [128, 4, NG], F32))
                se4 = el(nc.sbuf_tensor(tag + "se4", [128, 4, NG], F32))
                sS3 = el(nc.sbuf_tensor(tag + "sS3", [128, NG, 3], F32))
                sval = el(nc.sbuf_tensor(tag + "sval", [128, NG, 4], F32))
                smv = el(nc.sbuf_tensor(tag + "smv", [128, 4, NG], F32))
                srwA = el(nc.sbuf_tensor(tag + "srwA", [128, NG, 8], F32))
                st_a = el(nc.sbuf_tensor(tag + "st_a", [128, NG, 8], F32))
                st_b = el(nc.sbuf_tensor(tag + "st_b", [128, NG, 8], F32))
                st_c = el(nc.sbuf_tensor(tag + "st_c", [128, NG, 2], F32))
                sabs = el(nc.sbuf_tensor(tag + "sabs", [128, NG, 8], F32))
                sZ8 = el(nc.sbuf_tensor(tag + "sZ8", [128, NG], F32))
                szc = el(nc.sbuf_tensor(tag + "szc", [128, NG], F32))
                samx = el(nc.sbuf_tensor(tag + "samx", [128, NG], F32))
                sthn = el(nc.sbuf_tensor(tag + "sthn", [128, NG], F32))
                sthab = el(nc.sbuf_tensor(tag + "sthab", [128, NG], F32))
                se4b = el(nc.sbuf_tensor(tag + "se4b", [128, 4, NG], F32))
                srs = el(nc.sbuf_tensor(tag + "srs", [128, NG], F32))
                srsi = el(nc.sbuf_tensor(tag + "srsi", [128, NG], F32))
                scnt = el(nc.sbuf_tensor(tag + "scnt", [128, 8], F32))

                dyn = slog9[:, :, 0:8]
                B.v(lambda v: v.tensor_copy(smsk[:], dyn))
                B.vmemset(sno[:], 1.0)
                B.v(lambda v: v.tensor_reduce(sthr[:, 0, :], smsk[:], AX.X,
                                              ALU.max))
                tsub = B.v(lambda v: v.tensor_tensor(
                    st_a[:], dyn,
                    sthr[:, 0, :].unsqueeze(2).broadcast_to([128, NG, 8]),
                    ALU.subtract))
                te1 = B.a(lambda a: a.activation(se1[:], st_a[:], AF.Exp),
                          wait=(tsub,))
                B.v(lambda v: v.tensor_reduce(sZ8[:], se1[:], AX.X,
                                              ALU.add), wait=(te1,))
                B.v(lambda v: v.tensor_single_scalar(sabs[:], dyn, -1.0,
                                                     ALU.mult))
                B.v(lambda v: v.tensor_tensor(sabs[:], sabs[:], dyn, ALU.max))
                for r in range(4):
                    thr_bc = sthr[:, r, :].unsqueeze(2).broadcast_to(
                        [128, NG, 8])
                    B.v(lambda v, thr_bc=thr_bc: v.tensor_tensor(
                        st_a[:], smsk[:], thr_bc, ALU.is_equal))
                    B.v(lambda v: v.tensor_tensor(
                        st_a[:], st_a[:],
                        s8e[:].unsqueeze(1).broadcast_to([128, NG, 8]),
                        ALU.mult))
                    B.v(lambda v: v.tensor_reduce(samx[:], st_a[:], AX.X,
                                                  ALU.max))
                    B.v(lambda v, r=r: v.tensor_tensor(
                        scf[:, :, r, :], st_a[:],
                        samx[:].unsqueeze(2).broadcast_to([128, NG, 8]),
                        ALU.is_equal))
                    B.v(lambda v, r=r: v.tensor_single_scalar(
                        sthn[:], sthr[:, r, :], -1.0, ALU.mult))
                    B.v(lambda v, r=r: v.tensor_tensor(
                        sthab[:], sthn[:], sthr[:, r, :], ALU.max))
                    B.v(lambda v: v.tensor_tensor(
                        st_a[:], sabs[:],
                        sthab[:].unsqueeze(2).broadcast_to([128, NG, 8]),
                        ALU.max))
                    B.v(lambda v: v.tensor_single_scalar(st_a[:], st_a[:], 0.2,
                                                         ALU.mult))
                    B.v(lambda v, thr_bc=thr_bc: v.tensor_tensor(
                        st_b[:], thr_bc, dyn, ALU.subtract))
                    B.v(lambda v: v.tensor_tensor(st_b[:], st_b[:], st_a[:],
                                                  ALU.is_le))
                    B.v(lambda v: v.tensor_tensor(st_b[:], st_b[:], sno[:],
                                                  ALU.mult))
                    B.v(lambda v: v.tensor_tensor(st_b[:], st_b[:], se1[:],
                                                  ALU.mult))
                    B.v(lambda v, r=r: v.tensor_reduce(sD[:, r, :], st_b[:],
                                                       AX.X, ALU.add))
                    B.v(lambda v, r=r: v.scalar_tensor_tensor(
                        sno[:], scf[:, :, r, :], -1.0, sno[:], ALU.mult,
                        ALU.add))
                    B.v(lambda v, r=r: v.scalar_tensor_tensor(
                        smsk[:], scf[:, :, r, :], -1e30, smsk[:], ALU.mult,
                        ALU.add))
                    if r < 3:
                        B.v(lambda v, r=r: v.tensor_reduce(
                            sthr[:, r + 1, :], smsk[:], AX.X, ALU.max))
                tsub4 = B.v(lambda v: v.tensor_tensor(
                    se4b[:], sthr[:],
                    sthr[:, 0:1, :].broadcast_to([128, 4, NG]), ALU.subtract))
                te4 = B.a(lambda a: a.activation(se4[:], se4b[:], AF.Exp),
                          wait=(tsub4,))
                sc1, sc2, sc3 = sthn, sthab, samx
                B.v(lambda v: v.tensor_copy(sc1[:], se4[:, 0, :]),
                    wait=(te4,))
                B.v(lambda v: v.tensor_tensor(sc2[:], sc1[:], se4[:, 1, :],
                                              ALU.add))
                B.v(lambda v: v.tensor_tensor(sc3[:], sc2[:], se4[:, 2, :],
                                              ALU.add))
                B.v(lambda v: v.tensor_single_scalar(szc[:], sZ8[:], 0.75,
                                                     ALU.mult))
                B.v(lambda v: v.reciprocal(sD[:], sD[:]))
                cums = [None, sc1, sc2, sc3]
                for r in range(4):
                    mr = el(nc.sbuf_tensor(tag + "smr%d" % r, [128, NG], F32))
                    B.v(lambda v, r=r, mr=mr: v.tensor_tensor(
                        mr[:], se4[:, r, :], sD[:, r, :], ALU.mult))
                    if r > 0:
                        vr = el(nc.sbuf_tensor(tag + "svr%d" % r, [128, NG],
                                               F32))
                        mr2 = el(nc.sbuf_tensor(tag + "smq%d" % r, [128, NG],
                                                F32))
                        B.v(lambda v, r=r, vr=vr: v.tensor_tensor(
                            vr[:], cums[r][:], szc[:], ALU.is_lt))
                        B.v(lambda v, mr=mr, vr=vr, mr2=mr2: v.tensor_tensor(
                            mr2[:], mr[:], vr[:], ALU.mult))
                        mr = mr2
                    B.v(lambda v, r=r, mr=mr: v.tensor_copy(
                        smv[:, r, :], mr[:]))
                    if r == 0:
                        B.v(lambda v, r=r, mr=mr: v.tensor_tensor(
                            srwA[:], scf[:, :, r, :],
                            mr[:].unsqueeze(2).broadcast_to([128, NG, 8]),
                            ALU.mult))
                    else:
                        B.v(lambda v, r=r, mr=mr: v.tensor_tensor(
                            st_a[:], scf[:, :, r, :],
                            mr[:].unsqueeze(2).broadcast_to([128, NG, 8]),
                            ALU.mult))
                        B.v(lambda v: v.tensor_add(srwA[:], srwA[:], st_a[:]))
                tpre = B.v(lambda v: v.tensor_single_scalar(spre[:], srwA[:],
                                                            0.0, ALU.is_gt))
                B.v(lambda v: v.tensor_reduce(srs[:], srwA[:], AX.X,
                                              ALU.add))
                B.v(lambda v: v.reciprocal(srsi[:], srs[:]))
                B.v(lambda v: v.tensor_tensor(
                    srwn[:], srwA[:],
                    srsi[:].unsqueeze(2).broadcast_to([128, NG, 8]), ALU.mult))
                tcn = B.t(lambda t: t.matmul(
                    p8[:, 6, 0:256], ones_f[:],
                    spre[:].rearrange("p g e -> p (g e)"), start=True,
                    stop=True, skip_group_check=True), wait=(tpre,))
                tcv = B.v(lambda v: v.tensor_reduce(
                    scnt[:],
                    p8[:, 6, 0:256].rearrange("p (g e) -> p g e",
                                              e=8).transpose([0, 2, 1]),
                    AX.X, ALU.add), wait=(tcn,))
                B.serial_v = False
                tccin = B.dma("sync", ccin_ap[cc_i], scnt[0:1, :],
                              wait=(tcv,))
                if cc_i == 0 and dbg:
                    wtok = ("vE", B.val["vE"])
                    for nm, src in (("dbg_thr", sthr[:]), ("dbg_D", sD[:]),
                                    ("dbg_e1", se1[:]), ("dbg_cf", scf[:]),
                                    ("dbg_val", sval[:]), ("dbg_smv", smv[:]),
                                    ("dbg_srw", srwA[:]),
                                    ("dbg_se4", se4[:]),
                                    ("dbg_se4b", se4b[:])):
                        if nm in dbg:
                            B.dma("sync", dbg[nm], src, wait=(wtok,))
                B.flush(skip_q=("qG",) if build_wcache else ())
            return slog9, sl8, srwn, spre, wc_tok, tccin

        def emit_gather(tag, slog9, sl8, srwn, spre, cc_i, w_exp, tccin):
            onehot, tri_f, idf, prefm, ones_f = B.kconsts
            B.phase()
            B.serial_v = True
            with ExitStack() as cl:
                el = cl.enter_context
                skept = el(nc.sbuf_tensor(tag + "skpt", [128, NG, 8], F32))
                soff = el(nc.sbuf_tensor(tag + "soff", [128, 2, 8], F32))
                st8 = el(nc.sbuf_tensor(tag + "st8", [128, 2, 8], F32))
                sg1 = el(nc.sbuf_tensor(tag + "sg1", [128, NG, 2], F32))
                sm9 = el(nc.sbuf_tensor(tag + "sm9", [128, NG], F32))
                sm9b = el(nc.sbuf_tensor(tag + "sm9b", [128, NG], F32))
                sSd = el(nc.sbuf_tensor(tag + "sSd", [128, NG], F32))
                sden = el(nc.sbuf_tensor(tag + "sden", [128, NG], F32))
                sdinv = el(nc.sbuf_tensor(tag + "sdiv", [128, NG], F32))
                sfd = el(nc.sbuf_tensor(tag + "sfd", [128, NG], F32))
                sgsh = el(nc.sbuf_tensor(tag + "sgsh", [128, NG], F32))
                sw8 = el(nc.sbuf_tensor(tag + "sw8", [128, NG, 8], F32))
                sz8 = el(nc.sbuf_tensor(tag + "sz8", [128, NG], F32))
                sz8d = el(nc.sbuf_tensor(tag + "sz8d", [128, NG], F32))
                sd9 = el(nc.sbuf_tensor(tag + "sd9", [128, NG, 9], F32))
                sz9 = el(nc.sbuf_tensor(tag + "sz9", [128, NG, 9], F32))
                w12 = el(nc.sbuf_tensor(tag + "w12", [128, NG, 12], F32))
                p8 = el(nc.psum_tensor(tag + "p8g", [128, 8, TT], F32))

                tcc = B.g(lambda g: g.collective_compute(
                    "AllGather", ALU.bypass,
                    replica_groups=[list(range(NCORE))],
                    ins=[ccin_ap[cc_i]], outs=[ccout_ap[cc_i]]),
                    wait=(tccin,))
                tscc = B.dma("sync", scc[:], ccout_ap[cc_i], wait=(tcc,))
                tpp = B.t(lambda t: t.matmul(p8[:, 4, 0:8], prefm[:], scc[:],
                                             start=True, stop=True,
                                             skip_group_check=True),
                          wait=(tscc,))
                tof = B.v(lambda v: v.tensor_copy(soff[:, 0, :], p8[:, 4, 0:8]),
                          wait=(tpp,))
                bank_rel = {}
                for g in range(NG):
                    ba, bb = g % 2, 2 + g % 2
                    cur, nxt = g % 2, 1 - g % 2
                    t1 = B.t(lambda t, g=g, ba=ba: t.matmul(
                        p8[:, ba, 0:8], tri_f[:], spre[:, g, :], start=True,
                        stop=True, skip_group_check=True),
                        wait=(bank_rel.get(ba), tof))
                    t2 = B.t(lambda t, g=g, bb=bb: t.matmul(
                        p8[:, bb, 0:8], ones_f[:], spre[:, g, :], start=True,
                        stop=True, skip_group_check=True),
                        wait=(bank_rel.get(bb),))
                    t3 = B.v(lambda v, ba=ba, cur=cur: v.tensor_tensor(
                        st8[:, 0, :], p8[:, ba, 0:8], soff[:, cur, :],
                        ALU.add), wait=(t1,))
                    bank_rel[ba] = t3
                    B.v(lambda v: v.tensor_single_scalar(
                        st8[:, 1, :], st8[:, 0, :], CAP, ALU.is_le))
                    B.v(lambda v, g=g: v.tensor_tensor(
                        skept[:, g, :], st8[:, 1, :], spre[:, g, :], ALU.mult))
                    t4 = B.v(lambda v, bb=bb, cur=cur, nxt=nxt: v.tensor_tensor(
                        soff[:, nxt, :], soff[:, cur, :], p8[:, bb, 0:8],
                        ALU.add), wait=(t2,))
                    bank_rel[bb] = t4
                B.v(lambda v: v.tensor_reduce(sm9[:], slog9[:, :, 0:8],
                                              AX.X, ALU.max))
                B.v(lambda v: v.tensor_tensor(sm9b[:], sm9[:],
                                              sl8[:], ALU.max))
                tsd8 = B.v(lambda v: v.tensor_tensor(
                    sd9[:, :, 0:8], slog9[:, :, 0:8],
                    sm9b[:].unsqueeze(2).broadcast_to([128, NG, 8]),
                    ALU.subtract))
                tsl8 = B.v(lambda v: v.tensor_tensor(sz8d[:], sl8[:], sm9b[:],
                                                     ALU.subtract))
                tz = B.a(lambda a: a.activation(sz9[:, :, 0:8],
                                                sd9[:, :, 0:8], AF.Exp),
                         wait=(tsd8,))
                tz8 = B.a(lambda a: a.activation(sz8[:], sz8d[:], AF.Exp),
                          wait=(tsl8,))
                B.v(lambda v: v.tensor_tensor(sd9[:, :, 0:8], sz9[:, :, 0:8],
                                              skept[:], ALU.mult), wait=(tz,))
                B.v(lambda v: v.tensor_reduce(sSd[:], sd9[:, :, 0:8],
                                              AX.X, ALU.add))
                B.v(lambda v: v.tensor_tensor(sden[:], sSd[:],
                                              sz8[:], ALU.add), wait=(tz8,))
                B.v(lambda v: v.reciprocal(sdinv[:], sden[:]))
                B.v(lambda v: v.tensor_single_scalar(sfd[:], sSd[:], 0.5,
                                                     ALU.mult))
                B.v(lambda v: v.tensor_tensor(sfd[:], sfd[:], sdinv[:],
                                              ALU.mult))
                B.v(lambda v: v.scalar_tensor_tensor(
                    sgsh[:], sz8[:], 0.5, sdinv[:], ALU.mult,
                    ALU.mult))
                B.v(lambda v: v.tensor_tensor(
                    sw8[:], srwn[:],
                    sfd[:].unsqueeze(2).broadcast_to([128, NG, 8]), ALU.mult))
                B.v(lambda v: v.tensor_tensor(w12[:, :, 0:8], sw8[:],
                                              skept[:], ALU.mult))
                tw12 = B.v(lambda v: v.tensor_copy(
                    w12[:, :, 8:12],
                    sgsh[:].unsqueeze(2).broadcast_to([128, NG, 4])))
                prel = {}
                for g in range(NG):
                    pb = 5 + g % 2
                    tt1 = B.t(lambda t, g=g, pb=pb: t.transpose(
                        p8[0:12, pb, 0:128], w12[:, g, :], idf[:]),
                        wait=(tw12, prel.get(pb)))
                    prel[pb] = B.v(lambda v, g=g, pb=pb: v.tensor_copy(
                        w_exp[:, g * 128:(g + 1) * 128], p8[0:12, pb, 0:128]),
                        wait=(tt1,))
                B.serial_v = False
                if cc_i == 0 and dbg:
                    wtok = ("vE", B.val["vE"])
                    for nm, src in (("dbg_wexp", w_exp[:]),
                                    ("dbg_log", slog9[:]),
                                    ("dbg_rwn", srwn[:]),
                                    ("dbg_pre", spre[:]),
                                    ("dbg_kept", skept[:])):
                        if nm in dbg:
                            B.dma("sync", dbg[nm], src, wait=(wtok,))
                B.flush()

        def emit_ffn(tag, src_res_ap, dst_ap, wexp_tok, w_exp, wc_tok=None):
            onehot, tri_f, idf, prefm, ones_f = B.kconsts
            if wc_tok is None:
                wc_tok = {}
            B.phase()
            with ExitStack() as cl:
                el = cl.enter_context
                sxln = el(nc.sbuf_tensor(tag + "xln", [128, 2, KC, TT], BF16))
                sxres = el(nc.sbuf_tensor(tag + "res", [128, KC, TT], F32))
                sxout = el(nc.sbuf_tensor(tag + "out", [128, KC, TT], F32))
                swg = el(nc.sbuf_tensor(tag + "wg", [128, 2, KC, 512], BF16))
                swu = el(nc.sbuf_tensor(tag + "wu", [128, 2, KC, 512], BF16))
                swd = el(nc.sbuf_tensor(tag + "wd", [128, 2, NIC, C], BF16))
                ssil = el(nc.sbuf_tensor(tag + "sil", [128, 3, TT], F32))
                sst = el(nc.sbuf_tensor(tag + "sst", [128, 3, TT], F32))
                sa = el(nc.sbuf_tensor(tag + "sa", [128, NU, NIC, TT], BF16))
                p8 = el(nc.psum_tensor(tag + "p8f", [128, 8, TT], F32))

                rel_ln = [None, None]
                rel_res = [None]
                rel_out = [None]
                rel_wg = [None, None]
                rel_wu = [None, None]
                rel_wd = [None, None]
                bank_rel = {b: None for b in range(8)}
                a_rel = None
                for tile in range(NTILE):
                    sq, j = divmod(tile, 4)
                    bf = tile % 2
                    tln = B.dma("sync", sxln[:, bf, :, :],
                                fm_tile(xln_ap, sq, j), wait=(rel_ln[bf],))
                    tres = B.dma("sync", sxres[:],
                                 fm_tile(src_res_ap, sq, j),
                                 wait=(rel_res[0],))
                    a_tok = {}
                    tg = tu = None
                    for e in range(NU):
                        ub = e % 2
                        twg = B.dma("gpsimd", swg[:, ub, :, :],
                                    cwg_ap[e].rearrange("(kc p) i -> p kc i",
                                                        p=128),
                                    wait=(rel_wg[ub],
                                          wc_tok.get((True, False, e))))
                        twu = B.dma("gpsimd", swu[:, ub, :, :],
                                    cwu_ap[e].rearrange("(kc p) i -> p kc i",
                                                        p=128),
                                    wait=(rel_wu[ub],
                                          wc_tok.get((False, True, e))))
                        wb = 6 + e % 2
                        tw = B.t(lambda t, e=e, wb=wb, tile=tile: t.matmul(
                            p8[:, wb, :], onehot[:, e, :],
                            w_exp[:, tile * TT:(tile + 1) * TT], start=True,
                            stop=True, skip_group_check=True),
                            wait=(wexp_tok, bank_rel[wb]))
                        tpw = None
                        for ic in range(NIC):
                            pb = (4 * e + ic) % 3
                            tg = B.mm_group(
                                lambda pb=pb: p8[:, pb, :],
                                [(swg[:, ub, kc, ic * 128:(ic + 1) * 128],
                                  sxln[:, bf, kc, :]) for kc in range(KC)],
                                wait_first=(twg, tln, bank_rel[pb]))
                            tu = B.mm_group(
                                lambda pb=pb: p8[:, 3 + pb, :],
                                [(swu[:, ub, kc, ic * 128:(ic + 1) * 128],
                                  sxln[:, bf, kc, :]) for kc in range(KC)],
                                wait_first=(twu, bank_rel[3 + pb]))
                            ta = B.a(lambda a, pb=pb: a.activation(
                                ssil[:, pb, :], p8[:, pb, :], AF.Silu),
                                wait=(tg,))
                            bank_rel[pb] = ta
                            t1 = B.v(lambda v, pb=pb: v.tensor_tensor(
                                sst[:, pb, :], ssil[:, pb, :],
                                p8[:, 3 + pb, :], ALU.mult), wait=(ta, tu))
                            bank_rel[3 + pb] = t1
                            t2 = B.v(lambda v, e=e, ic=ic, pb=pb, wb=wb:
                                     v.tensor_tensor(
                                         sa[:, e, ic, :], sst[:, pb, :],
                                         p8[:, wb, :], ALU.mult),
                                     wait=(t1, tw, a_rel))
                            tpw = t2
                            a_tok[(e, ic)] = t2
                        bank_rel[wb] = tpw
                        rel_wg[ub] = tg
                        rel_wu[ub] = tu
                    rel_ln[bf] = tu
                    down_stop = {}
                    tl = None
                    for e in range(NU):
                        db = e % 2
                        twd = B.dma("gpsimd", swd[:, db, :, :],
                                    cwd_ap[e].rearrange("(ic p) h -> p ic h",
                                                        p=128),
                                    wait=(rel_wd[db],
                                          wc_tok.get((False, False, e))))
                        for hc in range(8):
                            for ic in range(NIC):
                                w = [twd, a_tok[(e, ic)]]
                                if e == 0 and ic == 0:
                                    w.append(bank_rel[hc])
                                sp = (e == NU - 1 and ic == NIC - 1)
                                last_of_unit = (hc == 7 and ic == NIC - 1)
                                tl = B.op(
                                    "tensor",
                                    lambda t, hc=hc, ic=ic, db=db,
                                    st=(e == 0 and ic == 0), sp=sp, e=e:
                                    t.matmul(
                                        p8[:, hc, :],
                                        swd[:, db, ic, hc * 128:(hc + 1) * 128],
                                        sa[:, e, ic, :], start=st, stop=sp,
                                        skip_group_check=True),
                                    wait=tuple(w),
                                    inc="tE" if (sp or last_of_unit) else None)
                                if sp:
                                    down_stop[hc] = tl
                                if last_of_unit:
                                    rel_wd[db] = tl
                    a_rel = down_stop[7]
                    evs = []
                    for hc in range(8):
                        tev = B.v(lambda v, hc=hc: v.tensor_tensor(
                            sxout[:, hc, :], p8[:, hc, :],
                            sxres[:, hc, :], ALU.add),
                            wait=(down_stop[hc], tres, rel_out[0]))
                        bank_rel[hc] = tev
                        evs.append(tev)
                    rel_res[0] = evs[-1]
                    rel_out[0] = B.dma("sync", fm_tile(dst_ap, sq, j),
                                       sxout[:], wait=tuple(evs))
                    if tile == 0 and dbg and tag == "f1":
                        if "dbg_a" in dbg:
                            B.dma("sync", dbg["dbg_a"], sa[:],
                                  wait=(a_rel,))
                        if "dbg_x1t" in dbg:
                            B.dma("sync", dbg["dbg_x1t"], sxout[:],
                                  wait=tuple(evs))
                B.flush()

        def emit_attn():
            B.phase()
            with ExitStack() as cl:
                el = cl.enter_context
                sx1t = el(nc.sbuf_tensor("a_x1", [128, 2, KC, TT], F32))
                sxl2 = el(nc.sbuf_tensor("a_ln", [128, 2, KC, TT], BF16))
                sstat = el(nc.sbuf_tensor("a_st", [128, 2, TT], F32))
                std_ = el(nc.sbuf_tensor("a_sd", [128, TT], F32))
                sK = el(nc.sbuf_tensor("a_K", [128, KC, S], BF16))
                sQ = el(nc.sbuf_tensor("a_Q", [128, KC, S], BF16))
                sV = el(nc.sbuf_tensor("a_V", [128, S // 128, C], BF16))
                swv = el(nc.sbuf_tensor("a_wv", [128, KC, TT], BF16))
                swx = el(nc.sbuf_tensor("a_wx", [128, 2, KC, 128], BF16))
                sPT = el(nc.sbuf_tensor("a_PT", [128, 8, TT], BF16))
                sO = el(nc.sbuf_tensor("a_O", [128, KC, TT], BF16))
                srinv = el(nc.sbuf_tensor("a_ri", [128, TT], F32))
                sln = el(nc.sbuf_tensor("a_sln", [128, TT], F32))
                sx2t = el(nc.sbuf_tensor("a_x2", [128, KC, TT], F32))
                p8 = el(nc.psum_tensor("a_p8", [128, 8, TT], F32))

                bank_rel = {b: None for b in range(8)}
                rel_x1 = [None, None]
                rel_ln2 = [None, None]
                t6_prev = [None]
                pending = [None]
                sqv_rel = [None]
                relw = [None, None]
                o_rel = [None]
                rinv_rel = [None]
                x2_rel = [None]
                kq_tok = {}
                v_tok = {}
                exp_toks = {}

                slot_state = {"i": 0}
                NSLOT = 8
                slot_rel = [None] * NSLOT

                def make_head(job, prow, b0, av, sm):
                    g, h, n = job
                    base = 64 * (h % 2)
                    oc = h // 2
                    pt = h % 2
                    slots = []
                    ex_toks = []
                    sum_tok = [None]

                    def s_chunk(c):
                        pb = b0 + (c % 2)
                        diag = c >= 4 * g
                        B.op("tensor", lambda t, c=c, pb=pb: t.matmul(
                            p8[:, pb, :],
                            sK[base:base + 64, oc, c * 128:(c + 1) * 128],
                            sQ[base:base + 64, oc, g * TT:(g + 1) * TT],
                            start=True, stop=not diag,
                            skip_group_check=True),
                            wait=(kq_tok[(1, oc, c // 4)],
                                  kq_tok[(0, oc, g)], bank_rel[pb]),
                            inc=None if diag else "tE")
                        if diag:
                            return B.t(lambda t, c=c, pb=pb: t.matmul(
                                p8[:, pb, :], idb[:],
                                cmneg[:, c - 4 * g, :],
                                start=False, stop=True,
                                skip_group_check=True))
                        return ("tE", B.val["tE"])

                    def s_pair(k):
                        t0 = s_chunk(2 * k)
                        t1 = s_chunk(2 * k + 1)
                        sl = slot_state["i"] % NSLOT
                        slot_state["i"] += 2
                        slots.append(sl)
                        return (t0, t1, sl)

                    def set_exp(tex):
                        bank_rel[b0] = tex
                        bank_rel[b0 + 1] = tex
                        ex_toks.append(tex)

                    def tails_pair(k):
                        sl = slots[k]
                        tex = ex_toks[k]
                        last = (2 * k + 1 == n - 1)
                        for ci in range(2):
                            c = 2 * k + ci
                            w = [tex]
                            if c == 0:
                                w.append(bank_rel[sm])
                            tsm = B.op("tensor", lambda t, sl=sl, ci=ci:
                                       t.matmul(
                                           p8[prow:prow + 64, sm, :],
                                           ones_bf[:, 0:64],
                                           sPT[:, sl + ci, :],
                                           start=(2 * k + ci == 0),
                                           stop=(2 * k + ci == n - 1),
                                           skip_group_check=True),
                                       wait=tuple(w),
                                       inc="tE" if 2 * k + ci == n - 1
                                       else None)
                            if 2 * k + ci == n - 1:
                                sum_tok[0] = tsm
                        tav = None
                        for ci in range(2):
                            c = 2 * k + ci
                            w = [tex, v_tok[c]]
                            if c == 0:
                                w.append(bank_rel[av])
                            tav = B.op("tensor", lambda t, sl=sl, ci=ci, c=c:
                                       t.matmul(
                                           p8[prow:prow + 64, av, :],
                                           sV[:, c, 64 * h:64 * h + 64],
                                           sPT[:, sl + ci, :],
                                           start=(c == 0), stop=(c == n - 1),
                                           skip_group_check=True),
                                       wait=tuple(w), inc="tE")
                            slot_rel[sl + ci] = tav
                        return tav

                    def finish(tav, trinv):
                        tev = B.v(lambda v: v.tensor_tensor(
                            sO[base:base + 64, oc, :],
                            p8[prow:prow + 64, av, :],
                            srinv[prow:prow + 64, :],
                            ALU.mult),
                            wait=(tav, trinv, o_rel[0]))
                        bank_rel[av] = tev
                        return tev

                    return (n, s_pair, tails_pair, finish, sum_tok,
                            set_exp, slots)

                def emit_pair(jobA, jobB, pp):
                    av = 4 + (pp % 2)
                    sm = 4 + ((pp + 1) % 2)
                    (nA, sA, tA, fA, stA, seA,
                     slotsA) = make_head(jobA, 0, 0, av, sm)
                    (nB, sB, tB, fB, stB, seB,
                     slotsB) = make_head(jobB, 64, 2, av, sm)
                    npair = nA // 2

                    def exp_k(k):
                        tA0, tA1, slA = sA(k)
                        tB0, tB1, slB = sB(k)
                        assert slB == slA + 2
                        tex = B.a(lambda a, slA=slA: a.activation(
                            sPT[:, slA:slA + 4, :], p8[:, 0:4, :],
                            AF.Exp, bias=0.0, scale=ATTN_SCALE),
                            wait=(tA0, tA1, tB0, tB1,
                                  slot_rel[slA], slot_rel[slA + 1],
                                  slot_rel[slA + 2], slot_rel[slA + 3]))
                        seA(tex)
                        seB(tex)

                    exp_k(0)
                    if pending[0] is not None:
                        pending[0]()
                        pending[0] = None
                    for k in range(1, npair):
                        exp_k(k)
                        tA(k - 1)
                        tB(k - 1)
                    tavA = tA(npair - 1)
                    tavB = tB(npair - 1)

                    def finish_pair(done_cb):
                        def run():
                            rw = [stA[0], stB[0]]
                            if rinv_rel[0] is not None:
                                rw.extend(rinv_rel[0])
                            tl1 = B.a(lambda a: a.activation(
                                sln[:], p8[:, sm, :], AF.Ln, bias=0.0,
                                scale=1.0), wait=tuple(rw))
                            bank_rel[sm] = tl1
                            tl2 = B.a(lambda a: a.activation(
                                srinv[:], sln[:], AF.Exp, bias=0.0,
                                scale=-1.0))
                            tevA = fA(tavA, tl2)
                            tevB = fB(tavB, tl2)
                            rinv_rel[0] = (tevA, tevB)
                            done_cb(tevA, tevB)
                        return run
                    return finish_pair

                def emit_wo(sq, g, otoks):
                    tr = B.dma("sync", sx2t[:], fm_tile(x1_ap, sq, g),
                               wait=(x2_rel[0],))
                    evs = []
                    for oc in range(KC):
                        sb = oc % 2
                        twl = B.dma("gpsimd", swx[:, sb, :, :],
                                    cwq_ap[3].rearrange(
                                        "(kc p) o -> p kc o",
                                        p=128)[:, :, oc * 128:(oc + 1) * 128],
                                    wait=(relw[sb],))
                        pb = 6 + oc % 2
                        tpo = B.mm_group(
                            lambda pb=pb: p8[:, pb, :],
                            [(swx[:, sb, kc, :], sO[:, kc, :])
                             for kc in range(KC)],
                            wait_first=tuple([twl, bank_rel[pb]] + otoks))
                        relw[sb] = tpo
                        tev = B.v(lambda v, oc=oc, pb=pb: v.tensor_tensor(
                            sx2t[:, oc, :], sx2t[:, oc, :], p8[:, pb, :],
                            ALU.add), wait=(tpo, tr))
                        bank_rel[pb] = tev
                        evs.append(tev)
                    o_rel[0] = evs[-1]
                    x2_rel[0] = B.dma("sync", fm_tile(x2_ap, sq, g), sx2t[:],
                                      wait=tuple(evs))

                wv_rel = [None]
                for sq in range(BSEQ):
                    kq_tok.clear()
                    v_tok.clear()
                    def emit_ln(j):
                        bf = j % 2
                        tx = B.dma("sync", sx1t[:, bf, :, :],
                                   fm_tile(x1_ap, sq, j),
                                   wait=(rel_x1[bf],))
                        tb16 = B.v(lambda v, bf=bf: v.tensor_copy(
                            sxl2[:, bf, :, :], sx1t[:, bf, :, :]),
                            wait=(tx, rel_ln2[bf]))
                        sqv = sPT
                        tq2 = B.v(lambda v, bf=bf: v.tensor_tensor(
                            sPT[:, 0:KC, :], sxl2[:, bf, :, :],
                            sxl2[:, bf, :, :], ALU.mult),
                            wait=tuple([sqv_rel[0]] + list(slot_rel)))
                        return lambda: emit_ln_post(bf, tb16, tq2)

                    def emit_ln_post(bf, tb16, tq2):
                        tm = B.mm_group(
                            lambda: p8[:, 6, :],
                            [(ones_bf[:], sxl2[:, bf, kc, :])
                             for kc in range(KC)],
                            wait_first=(tb16, bank_rel[6]))
                        t1 = B.v(lambda v: v.tensor_single_scalar(
                            sstat[:, 0, :], p8[:, 6, :], 1.0 / C,
                            ALU.mult), wait=(tm,))
                        bank_rel[6] = t1
                        tmq = B.mm_group(
                            lambda: p8[:, 7, :],
                            [(ones_bf[:], sPT[:, kc, :])
                             for kc in range(KC)],
                            wait_first=(tq2, bank_rel[7]))
                        sqv_rel[0] = tmq
                        for _sl in range(NSLOT):
                            slot_rel[_sl] = tmq
                        t2 = B.v(lambda v: v.tensor_single_scalar(
                            sstat[:, 1, :], p8[:, 7, :], 1.0 / C,
                            ALU.mult), wait=(tmq,))
                        bank_rel[7] = t2
                        B.v(lambda v: v.scalar_tensor_tensor(
                            std_[:], sstat[:, 0, :], -1.0,
                            sstat[:, 0, :], ALU.mult, ALU.mult),
                            wait=(t6_prev[0],))
                        t4 = B.v(lambda v: v.tensor_add(
                            sstat[:, 1, :], sstat[:, 1, :], std_[:]))
                        B.a(lambda a: a.activation(
                            std_[:], sstat[:, 1, :], AF.Ln,
                            bias=seps[:], scale=1.0), wait=(t4,))
                        t6 = B.a(lambda a: a.activation(
                            sstat[:, 1, :], std_[:], AF.Exp,
                            bias=0.0, scale=-0.5))
                        t6_prev[0] = t6
                        tn = None
                        for kc in range(KC):
                            B.v(lambda v, bf=bf, kc=kc: v.tensor_sub(
                                std_[:], sx1t[:, bf, kc, :], sstat[:, 0, :]),
                                wait=(t6,) if kc == 0 else ())
                            tn = B.v(lambda v, bf=bf, kc=kc: v.tensor_tensor(
                                sxl2[:, bf, kc, :], std_[:], sstat[:, 1, :],
                                ALU.mult))
                        rel_x1[bf] = tn
                        return tn

                    def emit_qkv(j, tn, mid_post=None):
                        bf = j % 2
                        tn_new = None
                        for wi, dst in ((1, sK), (0, sQ)):
                            if wi == 0 and mid_post is not None:
                                tn_new = mid_post()
                            for oc in range(KC):
                                sb = oc % 2
                                twl = B.dma(
                                    "gpsimd", swx[:, sb, :, :],
                                    cwq_ap[wi].rearrange(
                                        "(kc p) o -> p kc o",
                                        p=128)[:, :, oc * 128:(oc + 1) * 128],
                                    wait=(relw[sb],))
                                pb = 4 if wi == 1 else 5
                                tpk = B.mm_group(
                                    lambda pb=pb: p8[:, pb, :],
                                    [(swx[:, sb, kc, :], sxl2[:, bf, kc, :])
                                     for kc in range(KC)],
                                    wait_first=(twl, tn, bank_rel[pb]))
                                relw[sb] = tpk
                                rel_ln2[bf] = tpk
                                tcp = B.v(lambda v, dst=dst, oc=oc, j=j, pb=pb:
                                          v.tensor_copy(
                                              dst[:, oc,
                                                  j * TT:(j + 1) * TT],
                                              p8[:, pb, :]), wait=(tpk,))
                                bank_rel[pb] = tcp
                                kq_tok[(wi, oc, j)] = tcp
                        for og in range(2):
                            twv = B.dma(
                                "gpsimd", swv[:],
                                cwq_ap[2].rearrange(
                                    "(kc p) o -> p kc o",
                                    p=128)[:, :, og * TT:(og + 1) * TT],
                                wait=(wv_rel[0],))
                            for tc in range(4):
                                tch = j * 4 + tc
                                pb = tc % 2
                                tpv = B.mm_group(
                                    lambda pb=pb: p8[:, pb, :],
                                    [(sxl2[:, bf, kc, tc * 128:(tc + 1) * 128],
                                      swv[:, kc, :])
                                     for kc in range(KC)],
                                    wait_first=(twv, tn, bank_rel[pb]))
                                rel_ln2[bf] = tpv
                                wv_rel[0] = tpv
                                tcv = B.v(lambda v, tch=tch, og=og, pb=pb:
                                          v.tensor_copy(
                                              sV[:, tch,
                                                 og * TT:(og + 1) * TT],
                                              p8[:, pb, :]), wait=(tpv,))
                                bank_rel[pb] = tcv
                                if og == 1:
                                    v_tok[tch] = tcv
                                else:
                                    v_tok[tch] = tcv
                        return tn_new

                    tn_prev = emit_ln(0)()
                    for j in range(1, 4):
                        post_new = emit_ln(j)
                        tn_prev = emit_qkv(j - 1, tn_prev, post_new)
                    emit_qkv(3, tn_prev)

                    o_done = {}

                    def mk_done(g):
                        def cb(ta, tb):
                            o_done.setdefault(g, []).extend([ta, tb])
                            if len(o_done[g]) == NH:
                                emit_wo(sq, g, o_done[g])
                        return cb

                    pp = 0
                    for g in range(4):
                        for h in range(0, NH, 2):
                            fin = emit_pair((g, h, 4 * g + 4),
                                            (g, h + 1, 4 * g + 4), pp)
                            pending[0] = fin(mk_done(g))
                            pp += 1
                    if pending[0] is not None:
                        pending[0]()
                        pending[0] = None
                B.flush()

        with ExitStack() as span1:
            w_exp = span1.enter_context(
                nc.sbuf_tensor("w_exp1", [12, TPC], BF16))
            (slog9, sl8, srwn, spre, wc_tok,
             tccin) = emit_route(span1, "r1", x_ap, True, 0)
            if _STOP != "P1":
                emit_gather("g1", slog9, sl8, srwn, spre, 0, w_exp, tccin)
                wexp_tok = ("vE", B.val["vE"])
                if _STOP != "P2":
                    emit_ffn("f1", x_ap, x1_ap, wexp_tok, w_exp, wc_tok)
        if _STOP is None:
            emit_attn()
            with ExitStack() as span2:
                w_exp = span2.enter_context(
                    nc.sbuf_tensor("w_exp2", [12, TPC], BF16))
                (slog9, sl8, srwn, spre, _wct,
                 tccin) = emit_route(span2, "r2", x2_ap, False, 1)
                emit_gather("g2", slog9, sl8, srwn, spre, 1, w_exp, tccin)
                wexp_tok = ("vE", B.val["vE"])
                emit_ffn("f2", x2_ap, out_ap, wexp_tok, w_exp)

        B.phase()
        for eng, q in B.queue_sem.items():
            B.ops[eng].append((((q, B.val[q]),), None, None, 0))
        B.flush()

    return nc


def _prep(inputs):
    f = np.float32

    def c(a):
        return np.ascontiguousarray(a, dtype=f)

    wqkvo = c(np.stack([inputs["L0_attn_Wq"].T, inputs["L0_attn_Wk"].T,
                        inputs["L0_attn_Wv"].T, inputs["L0_attn_Wo"].T]))
    dg = np.asarray(inputs["L0_ffn_dyn_gate_w"]).transpose(0, 2, 1)
    du = np.asarray(inputs["L0_ffn_dyn_up_w"]).transpose(0, 2, 1)
    dd = np.asarray(inputs["L0_ffn_dyn_down_w"]).transpose(0, 2, 1)
    sg_ = np.asarray(inputs["L0_ffn_sh_gate_w"])[0].T.reshape(
        1024, 4, 512).transpose(1, 0, 2)
    su_ = np.asarray(inputs["L0_ffn_sh_up_w"])[0].T.reshape(
        1024, 4, 512).transpose(1, 0, 2)
    sd_ = np.asarray(inputs["L0_ffn_sh_down_w"])[0].T.reshape(4, 512, 1024)
    wg = c(np.concatenate([dg, sg_], axis=0))
    wu = c(np.concatenate([du, su_], axis=0))
    wd = c(np.concatenate([dd, sd_], axis=0))
    gwt = c(inputs["L0_ffn_gate_w"].T)
    return wqkvo, wg, wu, wd, gwt


def kernel(**inputs):
    x = np.asarray(inputs["L0_attn_x"], dtype=np.float32)
    xT = np.ascontiguousarray(x.transpose(0, 2, 1))
    wqkvo, wg, wu, wd, gwt = _prep(inputs)
    nc = build_program()
    import ml_dtypes
    kco = np.zeros((12, NU, 128), dtype=ml_dtypes.bfloat16)
    for e in range(NU):
        kco[e, e, :] = 1.0
    iot = np.arange(128, dtype=np.float32)
    ktri = (iot[None, :] >= iot[:, None]).astype(np.float32)
    kid = np.eye(128, dtype=np.float32)
    in_maps = []
    for c in range(NCORE):
        in_maps.append({
            "x": np.ascontiguousarray(xT[2 * c:2 * c + 2]),
            "wqkvo": wqkvo, "wg": wg, "wu": wu, "wd": wd, "gwt": gwt,
            "cid": np.full((128, 1), float(c), dtype=np.float32),
            "kco": kco, "ktri": ktri, "kid": kid,
            "kpre": np.broadcast_to(
                (np.arange(8) < c)[:, None], (8, 128)
            ).astype(np.float32).copy(),
        })
    res = run_bass_kernel_spmd(nc, in_maps, core_ids=list(range(NCORE)))
    out = np.empty((16, S, C), dtype=np.float32)
    for c in range(NCORE):
        o = np.asarray(res.results[c]["out"], dtype=np.float32)
        out[2 * c:2 * c + 2] = o.transpose(0, 2, 1)
    globals()["_last_results"] = res
    return out


# version 24
# speedup vs baseline: 1.0884x; 1.0136x over previous
import numpy as np
from contextlib import ExitStack

import concourse.bass as bass
import concourse.mybir as mybir
from concourse.bass_utils import run_bass_kernel_spmd

F32 = mybir.dt.float32
BF16 = mybir.dt.bfloat16
AF = mybir.ActivationFunctionType
ALU = mybir.AluOpType
AX = mybir.AxisListType

NCORE = 8
BSEQ = 2
S = 2048
C = 1024
KC = 8
NH = 16
HD = 64
NU = 12
NIC = 4
TT = 512
NTILE = 8
NG = 32
TPC = BSEQ * S
CAP = 5120.0
LNEPS = 1e-5
ATTN_SCALE = 0.125

_DEBUG = set()
_STOP = None

ENGS = ("sync", "gpsimd", "tensor", "vector", "scalar")


class Bld:

    def __init__(self, nc):
        self.nc = nc
        self.sems = {}
        self.val = {}
        self.waited = {e: {} for e in ENGS}
        self.ops = None
        self.queue_sem = {"sync": "qS", "gpsimd": "qG", "scalar": "qA"}

    def add_sem(self, name, handle):
        self.sems[name] = handle
        self.val[name] = 0

    def phase(self):
        self.ops = {e: [] for e in ENGS}

    def op(self, eng, fn, wait=(), inc=None, n=1):
        w = tuple(t for t in wait if t is not None)
        tok = None
        if inc is not None:
            self.val[inc] += n
            tok = (inc, self.val[inc])
        self.ops[eng].append((w, fn, inc, n))
        return tok

    def dma(self, eng, out, in_, wait=()):
        q = self.queue_sem[eng]
        return self.op(eng, lambda e, o=out, i=in_: e.dma_start(out=o, in_=i),
                       wait, inc=q, n=16)

    def t(self, fn, wait=()):
        return self.op("tensor", fn, wait, inc="tE")

    serial_v = False

    def v(self, fn, wait=()):
        tok = self.op("vector", fn, wait, inc="vE")
        if self.serial_v:
            self.ops["vector"].append(((tok,), None, None, 0))
        return tok

    def a(self, fn, wait=()):
        return self.op("scalar", fn, wait, inc="aE")

    def g(self, fn, wait=()):
        return self.op("gpsimd", fn, wait, inc="pE")

    def vmemset(self, ap, val):
        tok = self.v(lambda v, a=ap, c=val: v.memset(a, c))
        self.ops["vector"].append(((tok,), None, None, 0))
        return tok

    def mm_group(self, bank_fn, ops, wait_first=()):
        n = len(ops)
        tok = None
        for i, (lhsT, rhs) in enumerate(ops):
            w = wait_first if i == 0 else ()
            tok = self.op(
                "tensor",
                lambda t, bf=bank_fn, l=lhsT, r=rhs, st=(i == 0),
                sp=(i == n - 1): t.matmul(bf(), l, r, start=st, stop=sp,
                                          skip_group_check=True),
                w, inc="tE" if i == n - 1 else None)
        return tok

    def flush(self, skip_q=()):
        nc = self.nc
        for eng, q in self.queue_sem.items():
            if q in skip_q:
                continue
            if self.val.get(q, 0) > self.waited[eng].get(q, 0) and \
                    any(i == q for (_, _, i, _) in self.ops[eng]):
                self.ops[eng].append((((q, self.val[q]),), None, None, 0))
        with nc.Block() as block:
            for eng in ENGS:
                lst = self.ops[eng]
                if not lst:
                    continue
                getattr(block, eng)(self._body(eng, lst))
        self.ops = None

    def _body(self, eng, lst):
        def body(e):
            for (waits, fn, inc, n) in lst:
                for (sname, v) in waits:
                    if self.waited[eng].get(sname, 0) < v:
                        e.wait_ge(self.sems[sname], v)
                        self.waited[eng][sname] = v
                if fn is None:
                    continue
                inst = fn(e)
                if inc is not None:
                    inst.then_inc(self.sems[inc], n)
        return body


def fm_tile(ap3, s, j, w=TT):
    return ap3[s].rearrange("(kc p) t -> p kc t", p=128)[:, :, j * w:(j + 1) * w]


def build_program():
    nc = bass.Bass(target_bir_lowering=False)

    def dram(name, shape, dtype, kind=None):
        if kind is None and name in _DEBUG:
            kind = "ExternalOutput"
        if kind is None:
            return nc.dram_tensor(name, shape, dtype).ap()
        return nc.dram_tensor(name, shape, dtype, kind=kind).ap()

    x_ap = dram("x", [BSEQ, C, S], F32, "ExternalInput")
    wg_ap = dram("wg", [NU, C, 512], F32, "ExternalInput")
    wu_ap = dram("wu", [NU, C, 512], F32, "ExternalInput")
    wd_ap = dram("wd", [NU, 512, C], F32, "ExternalInput")
    wq4_ap = dram("wqkvo", [4, C, C], F32, "ExternalInput")
    gwt_ap = dram("gwt", [C, 9], F32, "ExternalInput")
    cid_ap = dram("cid", [128, 1], F32, "ExternalInput")
    kco_ap = dram("kco", [12, NU, 128], BF16, "ExternalInput")
    ktri_ap = dram("ktri", [128, 128], F32, "ExternalInput")
    kid_ap = dram("kid", [128, 128], F32, "ExternalInput")
    kpre_ap = dram("kpre", [8, 128], F32, "ExternalInput")
    out_ap = dram("out", [BSEQ, C, S], F32, "ExternalOutput")

    x1_ap = dram("x1", [BSEQ, C, S], F32)
    x2_ap = dram("x2", [BSEQ, C, S], F32)
    xln_ap = dram("xln", [BSEQ, C, S], BF16)
    cwg_ap = dram("cwg", [NU, C, 512], BF16)
    cwu_ap = dram("cwu", [NU, C, 512], BF16)
    cwd_ap = dram("cwd", [NU, 512, C], BF16)
    cwq_ap = dram("cwq", [4, C, C], BF16)
    dbg = {}
    for nm, shp, dt in (("dbg_wexp", [12, TPC], BF16),
                        ("dbg_log", [128, NG, 9], F32),
                        ("dbg_rwn", [128, NG, 8], F32),
                        ("dbg_pre", [128, NG, 8], F32),
                        ("dbg_kept", [128, NG, 8], F32),
                        ("dbg_thr", [128, 4, NG], F32),
                        ("dbg_D", [128, 4, NG], F32),
                        ("dbg_e1", [128, NG, 8], F32),
                        ("dbg_cf", [128, NG, 4, 8], F32),
                        ("dbg_val", [128, NG, 4], F32),
                        ("dbg_smv", [128, 4, NG], F32),
                        ("dbg_stc", [128, NG, 2], F32),
                        ("dbg_srw", [128, NG, 8], F32),
                        ("dbg_se4", [128, 4, NG], F32),
                        ("dbg_se4b", [128, 4, NG], F32),
                        ("dbg_a", [128, NU, NIC, TT], BF16),
                        ("dbg_x1t", [128, KC, TT], F32)):
        if nm in _DEBUG:
            dbg[nm] = dram(nm, shp, dt, "ExternalOutput")
    ccin_ap = [dram("ccin%d" % i, [1, 8], F32) for i in range(2)]
    ccout_ap = [dram("ccout%d" % i, [NCORE, 8], F32) for i in range(2)]

    B = Bld(nc)

    with ExitStack() as top:
        et = top.enter_context
        for sname in ("qS", "qG", "qA", "tE", "vE", "aE", "pE"):
            B.add_sem(sname, et(nc.semaphore(sname)))

        ones_bf = et(nc.sbuf_tensor("ones_bf", [128, 128], BF16))
        cmneg = et(nc.sbuf_tensor("cmneg", [128, 4, TT], BF16))
        idb = et(nc.sbuf_tensor("idb", [128, 128], BF16))
        s8e = et(nc.sbuf_tensor("s8e", [128, 8], F32))
        scid = et(nc.sbuf_tensor("scid", [128, 1], F32))
        scc = et(nc.sbuf_tensor("scc", [8, 8], F32))
        seps = et(nc.sbuf_tensor("seps", [128, 1], F32))

        B.phase()
        with ExitStack() as c0:
            e0 = c0.enter_context
            it_p = e0(nc.sbuf_tensor("it_p", [128, TT], F32))
            it_f = e0(nc.sbuf_tensor("it_f", [128, TT], F32))
            sdm = e0(nc.sbuf_tensor("sdm", [128, TT], F32))
            wstg0 = e0(nc.sbuf_tensor("wstg0", [128, 2, KC, 1024], BF16))

            tcid = B.dma("sync", scid[:], cid_ap)
            tip = B.g(lambda g: g.iota(it_p[:], [[0, TT]], channel_multiplier=1,
                                       allow_small_or_imprecise_dtypes=True))
            tif = B.g(lambda g: g.iota(it_f[:], [[1, TT]], channel_multiplier=0,
                                       allow_small_or_imprecise_dtypes=True))
            tie = B.g(lambda g: g.iota(s8e[:], [[1, 8]], channel_multiplier=0,
                                       allow_small_or_imprecise_dtypes=True))
            B.vmemset(seps[:], LNEPS)
            B.vmemset(ones_bf[:], 1.0)
            B.v(lambda v: v.tensor_tensor(idb[:], it_p[:, 0:128],
                                          it_f[:, 0:128], ALU.is_equal),
                wait=(tip, tif))
            tsd = B.v(lambda v: v.tensor_sub(sdm[:], it_f[:], it_p[:]),
                      wait=(tip, tif))
            for d in range(4):
                tlt = B.v(lambda v, d=d: v.tensor_single_scalar(
                    cmneg[:, d, :], sdm[:], float(128 * d), ALU.is_lt),
                    wait=(tsd,))
                B.v(lambda v, d=d: v.tensor_single_scalar(
                    cmneg[:, d, :], cmneg[:, d, :], -800.0, ALU.mult),
                    wait=(tlt,))
            tng = B.v(lambda v: v.tensor_single_scalar(
                s8e[:], s8e[:], -1.0, ALU.mult), wait=(tie,))
            B.v(lambda v: v.tensor_single_scalar(s8e[:], s8e[:], 8.0, ALU.add),
                wait=(tng,))
            relw = [None, None]
            for m in range(4):
                b = m % 2
                tl = B.dma("gpsimd", wstg0[:, b, :, :],
                           wq4_ap[m].rearrange("(kc p) o -> p kc o", p=128),
                           wait=(relw[b],))
                relw[b] = B.dma("gpsimd",
                                cwq_ap[m].rearrange("(kc p) o -> p kc o", p=128),
                                wstg0[:, b, :, :], wait=(tl,))
            B.flush()

        def emit_route(ctx, tag, src_ap, build_wcache, cc_i):
            e1 = ctx.enter_context
            slog9 = e1(nc.sbuf_tensor(tag + "slog9", [128, NG, 9], F32))
            sl8 = e1(nc.sbuf_tensor(tag + "sl8", [128, NG], F32))
            sgw = e1(nc.sbuf_tensor(tag + "sgw", [128, KC, 9], F32))
            srwn = e1(nc.sbuf_tensor(tag + "srwn", [128, NG, 8], F32))
            spre = e1(nc.sbuf_tensor(tag + "spre", [128, NG, 8], F32))
            wstg = e1(nc.sbuf_tensor(tag + "wstg", [128, 2, KC, TT],
                                     BF16)) if build_wcache else None
            onehot = e1(nc.sbuf_tensor(tag + "oh", [12, NU, 128], BF16))
            tri_f = e1(nc.sbuf_tensor(tag + "tri", [128, 128], F32))
            idf = e1(nc.sbuf_tensor(tag + "idf", [128, 128], F32))
            prefm = e1(nc.sbuf_tensor(tag + "pre", [8, 128], F32))
            ones_f = e1(nc.sbuf_tensor(tag + "onef", [128, 128], F32))
            B.phase()
            with ExitStack() as cl:
                el = cl.enter_context
                sxt = el(nc.sbuf_tensor(tag + "sxt", [128, 2, KC, TT], F32))
                sxb = el(nc.sbuf_tensor(tag + "sxb", [128, 2, KC, TT], BF16))
                sxq = el(nc.sbuf_tensor(tag + "sxq", [128, 2, KC, TT], BF16))
                sxln32 = el(nc.sbuf_tensor(tag + "sl32", [128, 2, KC, TT], F32))
                sxlnb = el(nc.sbuf_tensor(tag + "slb", [128, 2, KC, TT], BF16))
                sstat = el(nc.sbuf_tensor(tag + "stat", [128, 2, 4, TT], F32))
                std_ = el(nc.sbuf_tensor(tag + "std", [128, 2, TT], F32))
                p8 = el(nc.psum_tensor(tag + "p8r", [128, 8, TT], F32))

                tgw = B.dma("sync", sgw[:],
                            gwt_ap.rearrange("(kc p) e -> p kc e", p=128))
                B.dma("sync", onehot[:], kco_ap)
                B.dma("sync", tri_f[:], ktri_ap)
                B.dma("sync", idf[:], kid_ap)
                B.dma("sync", prefm[:], kpre_ap)
                B.kconsts = (onehot, tri_f, idf, prefm, ones_f)
                B.vmemset(ones_f[:], 1.0)
                rel_buf = [None, None]
                rel_lnb = [None, None]
                rel_wc = [None, None]
                wc_tok = {}
                btok = {}
                for tile in range(NTILE):
                    sq, j = divmod(tile, 4)
                    bf = tile % 2
                    tx = B.dma("sync", sxt[:, bf, :, :], fm_tile(src_ap, sq, j),
                               wait=(rel_buf[bf],))
                    txb = B.v(lambda v, bf=bf: v.tensor_copy(
                        sxb[:, bf, :, :], sxt[:, bf, :, :]), wait=(tx,))
                    txq = B.v(lambda v, bf=bf: v.tensor_tensor(
                        sxq[:, bf, :, :], sxt[:, bf, :, :], sxt[:, bf, :, :],
                        ALU.mult))
                    tm = B.mm_group(
                        lambda bf=bf: p8[:, 2 * bf, :],
                        [(ones_bf[:], sxb[:, bf, kc, :]) for kc in range(KC)],
                        wait_first=(txb, btok.get((bf, "sA"))))
                    tq = B.mm_group(
                        lambda bf=bf: p8[:, 2 * bf + 1, :],
                        [(ones_bf[:], sxq[:, bf, kc, :]) for kc in range(KC)],
                        wait_first=(txq, btok.get((bf, "sB"))))
                    btok[(bf, "sA")] = B.v(
                        lambda v, bf=bf: v.tensor_single_scalar(
                            sstat[:, bf, 0, :], p8[:, 2 * bf, :], 1.0 / C,
                            ALU.mult), wait=(tm,))
                    btok[(bf, "sB")] = B.v(
                        lambda v, bf=bf: v.tensor_single_scalar(
                            sstat[:, bf, 1, :], p8[:, 2 * bf + 1, :], 1.0 / C,
                            ALU.mult), wait=(tq,))
                    B.v(lambda v, bf=bf: v.scalar_tensor_tensor(
                        sstat[:, bf, 2, :], sstat[:, bf, 0, :], -1.0,
                        sstat[:, bf, 0, :], ALU.mult, ALU.mult))
                    t4 = B.v(lambda v, bf=bf: v.tensor_add(
                        sstat[:, bf, 2, :], sstat[:, bf, 2, :],
                        sstat[:, bf, 1, :]))
                    B.a(lambda a, bf=bf: a.activation(
                        sstat[:, bf, 3, :], sstat[:, bf, 2, :], AF.Ln,
                        bias=seps[:], scale=1.0), wait=(t4,))
                    t6 = B.a(lambda a, bf=bf: a.activation(
                        sstat[:, bf, 2, :], sstat[:, bf, 3, :], AF.Exp,
                        bias=0.0, scale=-0.5))
                    tn = None
                    for kc in range(KC):
                        B.v(lambda v, bf=bf, kc=kc: v.tensor_sub(
                            std_[:, bf, :], sxt[:, bf, kc, :],
                            sstat[:, bf, 0, :]),
                            wait=(t6,) if kc == 0 else ())
                        tn = B.v(lambda v, bf=bf, kc=kc: v.tensor_tensor(
                            sxln32[:, bf, kc, :], std_[:, bf, :],
                            sstat[:, bf, 2, :], ALU.mult))
                    tb = B.v(lambda v, bf=bf: v.tensor_copy(
                        sxlnb[:, bf, :, :], sxln32[:, bf, :, :]),
                        wait=(rel_lnb[bf],))
                    rel_lnb[bf] = B.dma("sync", fm_tile(xln_ap, sq, j),
                                        sxlnb[:, bf, :, :], wait=(tb,))
                    last_pl = None
                    for tc in range(4):
                        gidx = tile * 4 + tc
                        pb = 4 + (gidx % 2)
                        tlg = B.mm_group(
                            lambda pb=pb: p8[:, pb, 0:9],
                            [(sxln32[:, bf, kc, tc * 128:(tc + 1) * 128],
                              sgw[:, kc, :]) for kc in range(KC)],
                            wait_first=(tn, tgw, btok.get(("pl", pb))))
                        B.v(lambda v, gidx=gidx, pb=pb: v.tensor_copy(
                            slog9[:, gidx, :], p8[:, pb, 0:9]), wait=(tlg,))
                        last_pl = B.v(
                            lambda v, gidx=gidx, pb=pb: v.tensor_copy(
                                sl8[:, gidx:gidx + 1], p8[:, pb, 8:9]))
                        btok[("pl", pb)] = last_pl
                    rel_buf[bf] = last_pl

                    if build_wcache and tile < 6:
                        for k in range(2):
                            e = 2 * tile + k
                            wb = (2 * tile + k) % 2
                            for (src, dst, nch, ncol) in (
                                    (wg_ap, cwg_ap, 8, 512),
                                    (wu_ap, cwu_ap, 8, 512),
                                    (wd_ap, cwd_ap, 4, 1024)):
                                stg = wstg[:, wb, :, :].rearrange(
                                    "p a b -> p (a b)").rearrange(
                                    "p (c d) -> p c d", d=ncol)[:, 0:nch, :]
                                tl = B.dma("gpsimd", stg,
                                           src[e].rearrange("(a p) b -> p a b",
                                                            p=128),
                                           wait=(rel_wc[wb],))
                                rel_wc[wb] = B.dma(
                                    "gpsimd",
                                    dst[e].rearrange("(a p) b -> p a b", p=128),
                                    stg, wait=(tl,))
                                wc_tok[(dst is cwg_ap, dst is cwu_ap, e)] = \
                                    rel_wc[wb]

                B.serial_v = True
                smsk = el(nc.sbuf_tensor(tag + "smsk", [128, NG, 8], F32))
                se1 = el(nc.sbuf_tensor(tag + "se1", [128, NG, 8], F32))
                sno = el(nc.sbuf_tensor(tag + "sno", [128, NG, 8], F32))
                scf = el(nc.sbuf_tensor(tag + "scf", [128, NG, 4, 8], F32))
                sthr = el(nc.sbuf_tensor(tag + "sthr", [128, 4, NG], F32))
                sD = el(nc.sbuf_tensor(tag + "sD", [128, 4, NG], F32))
                se4 = el(nc.sbuf_tensor(tag + "se4", [128, 4, NG], F32))
                sS3 = el(nc.sbuf_tensor(tag + "sS3", [128, NG, 3], F32))
                sval = el(nc.sbuf_tensor(tag + "sval", [128, NG, 4], F32))
                smv = el(nc.sbuf_tensor(tag + "smv", [128, 4, NG], F32))
                srwA = el(nc.sbuf_tensor(tag + "srwA", [128, NG, 8], F32))
                st_a = el(nc.sbuf_tensor(tag + "st_a", [128, NG, 8], F32))
                st_b = el(nc.sbuf_tensor(tag + "st_b", [128, NG, 8], F32))
                st_c = el(nc.sbuf_tensor(tag + "st_c", [128, NG, 2], F32))
                sabs = el(nc.sbuf_tensor(tag + "sabs", [128, NG, 8], F32))
                sZ8 = el(nc.sbuf_tensor(tag + "sZ8", [128, NG], F32))
                szc = el(nc.sbuf_tensor(tag + "szc", [128, NG], F32))
                samx = el(nc.sbuf_tensor(tag + "samx", [128, NG], F32))
                sthn = el(nc.sbuf_tensor(tag + "sthn", [128, NG], F32))
                sthab = el(nc.sbuf_tensor(tag + "sthab", [128, NG], F32))
                se4b = el(nc.sbuf_tensor(tag + "se4b", [128, 4, NG], F32))
                srs = el(nc.sbuf_tensor(tag + "srs", [128, NG], F32))
                srsi = el(nc.sbuf_tensor(tag + "srsi", [128, NG], F32))
                scnt = el(nc.sbuf_tensor(tag + "scnt", [128, 8], F32))

                dyn = slog9[:, :, 0:8]
                B.v(lambda v: v.tensor_copy(smsk[:], dyn))
                B.vmemset(sno[:], 1.0)
                B.v(lambda v: v.tensor_reduce(sthr[:, 0, :], smsk[:], AX.X,
                                              ALU.max))
                tsub = B.v(lambda v: v.tensor_tensor(
                    st_a[:], dyn,
                    sthr[:, 0, :].unsqueeze(2).broadcast_to([128, NG, 8]),
                    ALU.subtract))
                te1 = B.a(lambda a: a.activation(se1[:], st_a[:], AF.Exp),
                          wait=(tsub,))
                B.v(lambda v: v.tensor_reduce(sZ8[:], se1[:], AX.X,
                                              ALU.add), wait=(te1,))
                B.v(lambda v: v.tensor_single_scalar(sabs[:], dyn, -1.0,
                                                     ALU.mult))
                B.v(lambda v: v.tensor_tensor(sabs[:], sabs[:], dyn, ALU.max))
                for r in range(4):
                    thr_bc = sthr[:, r, :].unsqueeze(2).broadcast_to(
                        [128, NG, 8])
                    B.v(lambda v, thr_bc=thr_bc: v.tensor_tensor(
                        st_a[:], smsk[:], thr_bc, ALU.is_equal))
                    B.v(lambda v: v.tensor_tensor(
                        st_a[:], st_a[:],
                        s8e[:].unsqueeze(1).broadcast_to([128, NG, 8]),
                        ALU.mult))
                    B.v(lambda v: v.tensor_reduce(samx[:], st_a[:], AX.X,
                                                  ALU.max))
                    B.v(lambda v, r=r: v.tensor_tensor(
                        scf[:, :, r, :], st_a[:],
                        samx[:].unsqueeze(2).broadcast_to([128, NG, 8]),
                        ALU.is_equal))
                    B.v(lambda v, r=r: v.tensor_single_scalar(
                        sthn[:], sthr[:, r, :], -1.0, ALU.mult))
                    B.v(lambda v, r=r: v.tensor_tensor(
                        sthab[:], sthn[:], sthr[:, r, :], ALU.max))
                    B.v(lambda v: v.tensor_tensor(
                        st_a[:], sabs[:],
                        sthab[:].unsqueeze(2).broadcast_to([128, NG, 8]),
                        ALU.max))
                    B.v(lambda v: v.tensor_single_scalar(st_a[:], st_a[:], 0.2,
                                                         ALU.mult))
                    B.v(lambda v, thr_bc=thr_bc: v.tensor_tensor(
                        st_b[:], thr_bc, dyn, ALU.subtract))
                    B.v(lambda v: v.tensor_tensor(st_b[:], st_b[:], st_a[:],
                                                  ALU.is_le))
                    B.v(lambda v: v.tensor_tensor(st_b[:], st_b[:], sno[:],
                                                  ALU.mult))
                    B.v(lambda v: v.tensor_tensor(st_b[:], st_b[:], se1[:],
                                                  ALU.mult))
                    B.v(lambda v, r=r: v.tensor_reduce(sD[:, r, :], st_b[:],
                                                       AX.X, ALU.add))
                    B.v(lambda v, r=r: v.scalar_tensor_tensor(
                        sno[:], scf[:, :, r, :], -1.0, sno[:], ALU.mult,
                        ALU.add))
                    B.v(lambda v, r=r: v.scalar_tensor_tensor(
                        smsk[:], scf[:, :, r, :], -1e30, smsk[:], ALU.mult,
                        ALU.add))
                    if r < 3:
                        B.v(lambda v, r=r: v.tensor_reduce(
                            sthr[:, r + 1, :], smsk[:], AX.X, ALU.max))
                tsub4 = B.v(lambda v: v.tensor_tensor(
                    se4b[:], sthr[:],
                    sthr[:, 0:1, :].broadcast_to([128, 4, NG]), ALU.subtract))
                te4 = B.a(lambda a: a.activation(se4[:], se4b[:], AF.Exp),
                          wait=(tsub4,))
                sc1, sc2, sc3 = sthn, sthab, samx
                B.v(lambda v: v.tensor_copy(sc1[:], se4[:, 0, :]),
                    wait=(te4,))
                B.v(lambda v: v.tensor_tensor(sc2[:], sc1[:], se4[:, 1, :],
                                              ALU.add))
                B.v(lambda v: v.tensor_tensor(sc3[:], sc2[:], se4[:, 2, :],
                                              ALU.add))
                B.v(lambda v: v.tensor_single_scalar(szc[:], sZ8[:], 0.75,
                                                     ALU.mult))
                B.v(lambda v: v.reciprocal(sD[:], sD[:]))
                cums = [None, sc1, sc2, sc3]
                for r in range(4):
                    mr = el(nc.sbuf_tensor(tag + "smr%d" % r, [128, NG], F32))
                    B.v(lambda v, r=r, mr=mr: v.tensor_tensor(
                        mr[:], se4[:, r, :], sD[:, r, :], ALU.mult))
                    if r > 0:
                        vr = el(nc.sbuf_tensor(tag + "svr%d" % r, [128, NG],
                                               F32))
                        mr2 = el(nc.sbuf_tensor(tag + "smq%d" % r, [128, NG],
                                                F32))
                        B.v(lambda v, r=r, vr=vr: v.tensor_tensor(
                            vr[:], cums[r][:], szc[:], ALU.is_lt))
                        B.v(lambda v, mr=mr, vr=vr, mr2=mr2: v.tensor_tensor(
                            mr2[:], mr[:], vr[:], ALU.mult))
                        mr = mr2
                    B.v(lambda v, r=r, mr=mr: v.tensor_copy(
                        smv[:, r, :], mr[:]))
                    if r == 0:
                        B.v(lambda v, r=r, mr=mr: v.tensor_tensor(
                            srwA[:], scf[:, :, r, :],
                            mr[:].unsqueeze(2).broadcast_to([128, NG, 8]),
                            ALU.mult))
                    else:
                        B.v(lambda v, r=r, mr=mr: v.tensor_tensor(
                            st_a[:], scf[:, :, r, :],
                            mr[:].unsqueeze(2).broadcast_to([128, NG, 8]),
                            ALU.mult))
                        B.v(lambda v: v.tensor_add(srwA[:], srwA[:], st_a[:]))
                tpre = B.v(lambda v: v.tensor_single_scalar(spre[:], srwA[:],
                                                            0.0, ALU.is_gt))
                B.v(lambda v: v.tensor_reduce(srs[:], srwA[:], AX.X,
                                              ALU.add))
                B.v(lambda v: v.reciprocal(srsi[:], srs[:]))
                B.v(lambda v: v.tensor_tensor(
                    srwn[:], srwA[:],
                    srsi[:].unsqueeze(2).broadcast_to([128, NG, 8]), ALU.mult))
                tcn = B.t(lambda t: t.matmul(
                    p8[:, 6, 0:256], ones_f[:],
                    spre[:].rearrange("p g e -> p (g e)"), start=True,
                    stop=True, skip_group_check=True), wait=(tpre,))
                tcv = B.v(lambda v: v.tensor_reduce(
                    scnt[:],
                    p8[:, 6, 0:256].rearrange("p (g e) -> p g e",
                                              e=8).transpose([0, 2, 1]),
                    AX.X, ALU.add), wait=(tcn,))
                B.serial_v = False
                tccin = B.dma("sync", ccin_ap[cc_i], scnt[0:1, :],
                              wait=(tcv,))
                if cc_i == 0 and dbg:
                    wtok = ("vE", B.val["vE"])
                    for nm, src in (("dbg_thr", sthr[:]), ("dbg_D", sD[:]),
                                    ("dbg_e1", se1[:]), ("dbg_cf", scf[:]),
                                    ("dbg_val", sval[:]), ("dbg_smv", smv[:]),
                                    ("dbg_srw", srwA[:]),
                                    ("dbg_se4", se4[:]),
                                    ("dbg_se4b", se4b[:])):
                        if nm in dbg:
                            B.dma("sync", dbg[nm], src, wait=(wtok,))
                B.flush(skip_q=("qG",) if build_wcache else ())
            return slog9, sl8, srwn, spre, wc_tok, tccin

        def emit_gather(tag, slog9, sl8, srwn, spre, cc_i, w_exp, tccin):
            onehot, tri_f, idf, prefm, ones_f = B.kconsts
            B.phase()
            B.serial_v = True
            with ExitStack() as cl:
                el = cl.enter_context
                skept = el(nc.sbuf_tensor(tag + "skpt", [128, NG, 8], F32))
                soff = el(nc.sbuf_tensor(tag + "soff", [128, 2, 8], F32))
                st8 = el(nc.sbuf_tensor(tag + "st8", [128, 2, 8], F32))
                sg1 = el(nc.sbuf_tensor(tag + "sg1", [128, NG, 2], F32))
                sm9 = el(nc.sbuf_tensor(tag + "sm9", [128, NG], F32))
                sm9b = el(nc.sbuf_tensor(tag + "sm9b", [128, NG], F32))
                sSd = el(nc.sbuf_tensor(tag + "sSd", [128, NG], F32))
                sden = el(nc.sbuf_tensor(tag + "sden", [128, NG], F32))
                sdinv = el(nc.sbuf_tensor(tag + "sdiv", [128, NG], F32))
                sfd = el(nc.sbuf_tensor(tag + "sfd", [128, NG], F32))
                sgsh = el(nc.sbuf_tensor(tag + "sgsh", [128, NG], F32))
                sw8 = el(nc.sbuf_tensor(tag + "sw8", [128, NG, 8], F32))
                sz8 = el(nc.sbuf_tensor(tag + "sz8", [128, NG], F32))
                sz8d = el(nc.sbuf_tensor(tag + "sz8d", [128, NG], F32))
                sd9 = el(nc.sbuf_tensor(tag + "sd9", [128, NG, 9], F32))
                sz9 = el(nc.sbuf_tensor(tag + "sz9", [128, NG, 9], F32))
                w12 = el(nc.sbuf_tensor(tag + "w12", [128, NG, 12], F32))
                p8 = el(nc.psum_tensor(tag + "p8g", [128, 8, TT], F32))

                tcc = B.g(lambda g: g.collective_compute(
                    "AllGather", ALU.bypass,
                    replica_groups=[list(range(NCORE))],
                    ins=[ccin_ap[cc_i]], outs=[ccout_ap[cc_i]]),
                    wait=(tccin,))
                tscc = B.dma("sync", scc[:], ccout_ap[cc_i], wait=(tcc,))
                tpp = B.t(lambda t: t.matmul(p8[:, 4, 0:8], prefm[:], scc[:],
                                             start=True, stop=True,
                                             skip_group_check=True),
                          wait=(tscc,))
                tof = B.v(lambda v: v.tensor_copy(soff[:, 0, :], p8[:, 4, 0:8]),
                          wait=(tpp,))
                bank_rel = {}
                for g in range(NG):
                    ba, bb = g % 2, 2 + g % 2
                    cur, nxt = g % 2, 1 - g % 2
                    t1 = B.t(lambda t, g=g, ba=ba: t.matmul(
                        p8[:, ba, 0:8], tri_f[:], spre[:, g, :], start=True,
                        stop=True, skip_group_check=True),
                        wait=(bank_rel.get(ba), tof))
                    t2 = B.t(lambda t, g=g, bb=bb: t.matmul(
                        p8[:, bb, 0:8], ones_f[:], spre[:, g, :], start=True,
                        stop=True, skip_group_check=True),
                        wait=(bank_rel.get(bb),))
                    t3 = B.v(lambda v, ba=ba, cur=cur: v.tensor_tensor(
                        st8[:, 0, :], p8[:, ba, 0:8], soff[:, cur, :],
                        ALU.add), wait=(t1,))
                    bank_rel[ba] = t3
                    B.v(lambda v: v.tensor_single_scalar(
                        st8[:, 1, :], st8[:, 0, :], CAP, ALU.is_le))
                    B.v(lambda v, g=g: v.tensor_tensor(
                        skept[:, g, :], st8[:, 1, :], spre[:, g, :], ALU.mult))
                    t4 = B.v(lambda v, bb=bb, cur=cur, nxt=nxt: v.tensor_tensor(
                        soff[:, nxt, :], soff[:, cur, :], p8[:, bb, 0:8],
                        ALU.add), wait=(t2,))
                    bank_rel[bb] = t4
                B.v(lambda v: v.tensor_reduce(sm9[:], slog9[:, :, 0:8],
                                              AX.X, ALU.max))
                B.v(lambda v: v.tensor_tensor(sm9b[:], sm9[:],
                                              sl8[:], ALU.max))
                tsd8 = B.v(lambda v: v.tensor_tensor(
                    sd9[:, :, 0:8], slog9[:, :, 0:8],
                    sm9b[:].unsqueeze(2).broadcast_to([128, NG, 8]),
                    ALU.subtract))
                tsl8 = B.v(lambda v: v.tensor_tensor(sz8d[:], sl8[:], sm9b[:],
                                                     ALU.subtract))
                tz = B.a(lambda a: a.activation(sz9[:, :, 0:8],
                                                sd9[:, :, 0:8], AF.Exp),
                         wait=(tsd8,))
                tz8 = B.a(lambda a: a.activation(sz8[:], sz8d[:], AF.Exp),
                          wait=(tsl8,))
                B.v(lambda v: v.tensor_tensor(sd9[:, :, 0:8], sz9[:, :, 0:8],
                                              skept[:], ALU.mult), wait=(tz,))
                B.v(lambda v: v.tensor_reduce(sSd[:], sd9[:, :, 0:8],
                                              AX.X, ALU.add))
                B.v(lambda v: v.tensor_tensor(sden[:], sSd[:],
                                              sz8[:], ALU.add), wait=(tz8,))
                B.v(lambda v: v.reciprocal(sdinv[:], sden[:]))
                B.v(lambda v: v.tensor_single_scalar(sfd[:], sSd[:], 0.5,
                                                     ALU.mult))
                B.v(lambda v: v.tensor_tensor(sfd[:], sfd[:], sdinv[:],
                                              ALU.mult))
                B.v(lambda v: v.scalar_tensor_tensor(
                    sgsh[:], sz8[:], 0.5, sdinv[:], ALU.mult,
                    ALU.mult))
                B.v(lambda v: v.tensor_tensor(
                    sw8[:], srwn[:],
                    sfd[:].unsqueeze(2).broadcast_to([128, NG, 8]), ALU.mult))
                B.v(lambda v: v.tensor_tensor(w12[:, :, 0:8], sw8[:],
                                              skept[:], ALU.mult))
                tw12 = B.v(lambda v: v.tensor_copy(
                    w12[:, :, 8:12],
                    sgsh[:].unsqueeze(2).broadcast_to([128, NG, 4])))
                prel = {}
                for g in range(NG):
                    pb = 5 + g % 2
                    tt1 = B.t(lambda t, g=g, pb=pb: t.transpose(
                        p8[0:12, pb, 0:128], w12[:, g, :], idf[:]),
                        wait=(tw12, prel.get(pb)))
                    prel[pb] = B.v(lambda v, g=g, pb=pb: v.tensor_copy(
                        w_exp[:, g * 128:(g + 1) * 128], p8[0:12, pb, 0:128]),
                        wait=(tt1,))
                B.serial_v = False
                if cc_i == 0 and dbg:
                    wtok = ("vE", B.val["vE"])
                    for nm, src in (("dbg_wexp", w_exp[:]),
                                    ("dbg_log", slog9[:]),
                                    ("dbg_rwn", srwn[:]),
                                    ("dbg_pre", spre[:]),
                                    ("dbg_kept", skept[:])):
                        if nm in dbg:
                            B.dma("sync", dbg[nm], src, wait=(wtok,))
                B.flush()

        def emit_ffn(tag, src_res_ap, dst_ap, wexp_tok, w_exp, wc_tok=None):
            onehot, tri_f, idf, prefm, ones_f = B.kconsts
            if wc_tok is None:
                wc_tok = {}
            B.phase()
            with ExitStack() as cl:
                el = cl.enter_context
                sxln = el(nc.sbuf_tensor(tag + "xln", [128, 2, KC, TT], BF16))
                sxres = el(nc.sbuf_tensor(tag + "res", [128, KC, TT], F32))
                sxout = el(nc.sbuf_tensor(tag + "out", [128, KC, TT], F32))
                swg = el(nc.sbuf_tensor(tag + "wg", [128, 2, KC, 512], BF16))
                swu = el(nc.sbuf_tensor(tag + "wu", [128, 2, KC, 512], BF16))
                swd = el(nc.sbuf_tensor(tag + "wd", [128, 2, NIC, C], BF16))
                ssil = el(nc.sbuf_tensor(tag + "sil", [128, 3, TT], F32))
                sst = el(nc.sbuf_tensor(tag + "sst", [128, 3, TT], F32))
                sa = el(nc.sbuf_tensor(tag + "sa", [128, NU, NIC, TT], BF16))
                p8 = el(nc.psum_tensor(tag + "p8f", [128, 8, TT], F32))

                rel_ln = [None, None]
                rel_res = [None]
                rel_out = [None]
                rel_wg = [None, None]
                rel_wu = [None, None]
                rel_wd = [None, None]
                bank_rel = {b: None for b in range(8)}
                a_rel = None
                for tile in range(NTILE):
                    sq, j = divmod(tile, 4)
                    bf = tile % 2
                    tln = B.dma("sync", sxln[:, bf, :, :],
                                fm_tile(xln_ap, sq, j), wait=(rel_ln[bf],))
                    tres = B.dma("sync", sxres[:],
                                 fm_tile(src_res_ap, sq, j),
                                 wait=(rel_res[0],))
                    a_tok = {}
                    tg = tu = None
                    for e in range(NU):
                        ub = e % 2
                        twg = B.dma("gpsimd", swg[:, ub, :, :],
                                    cwg_ap[e].rearrange("(kc p) i -> p kc i",
                                                        p=128),
                                    wait=(rel_wg[ub],
                                          wc_tok.get((True, False, e))))
                        twu = B.dma("gpsimd", swu[:, ub, :, :],
                                    cwu_ap[e].rearrange("(kc p) i -> p kc i",
                                                        p=128),
                                    wait=(rel_wu[ub],
                                          wc_tok.get((False, True, e))))
                        wb = 6 + e % 2
                        tw = B.t(lambda t, e=e, wb=wb, tile=tile: t.matmul(
                            p8[:, wb, :], onehot[:, e, :],
                            w_exp[:, tile * TT:(tile + 1) * TT], start=True,
                            stop=True, skip_group_check=True),
                            wait=(wexp_tok, bank_rel[wb]))
                        tpw = None
                        for ic in range(NIC):
                            pb = (4 * e + ic) % 3
                            tg = B.mm_group(
                                lambda pb=pb: p8[:, pb, :],
                                [(swg[:, ub, kc, ic * 128:(ic + 1) * 128],
                                  sxln[:, bf, kc, :]) for kc in range(KC)],
                                wait_first=(twg, tln, bank_rel[pb]))
                            tu = B.mm_group(
                                lambda pb=pb: p8[:, 3 + pb, :],
                                [(swu[:, ub, kc, ic * 128:(ic + 1) * 128],
                                  sxln[:, bf, kc, :]) for kc in range(KC)],
                                wait_first=(twu, bank_rel[3 + pb]))
                            ta = B.a(lambda a, pb=pb: a.activation(
                                ssil[:, pb, :], p8[:, pb, :], AF.Silu),
                                wait=(tg,))
                            bank_rel[pb] = ta
                            t1 = B.v(lambda v, pb=pb: v.tensor_tensor(
                                sst[:, pb, :], ssil[:, pb, :],
                                p8[:, 3 + pb, :], ALU.mult), wait=(ta, tu))
                            bank_rel[3 + pb] = t1
                            t2 = B.v(lambda v, e=e, ic=ic, pb=pb, wb=wb:
                                     v.tensor_tensor(
                                         sa[:, e, ic, :], sst[:, pb, :],
                                         p8[:, wb, :], ALU.mult),
                                     wait=(t1, tw, a_rel))
                            tpw = t2
                            a_tok[(e, ic)] = t2
                        bank_rel[wb] = tpw
                        rel_wg[ub] = tg
                        rel_wu[ub] = tu
                    rel_ln[bf] = tu
                    down_stop = {}
                    tl = None
                    for e in range(NU):
                        db = e % 2
                        twd = B.dma("gpsimd", swd[:, db, :, :],
                                    cwd_ap[e].rearrange("(ic p) h -> p ic h",
                                                        p=128),
                                    wait=(rel_wd[db],
                                          wc_tok.get((False, False, e))))
                        for hc in range(8):
                            for ic in range(NIC):
                                w = [twd, a_tok[(e, ic)]]
                                if e == 0 and ic == 0:
                                    w.append(bank_rel[hc])
                                sp = (e == NU - 1 and ic == NIC - 1)
                                last_of_unit = (hc == 7 and ic == NIC - 1)
                                tl = B.op(
                                    "tensor",
                                    lambda t, hc=hc, ic=ic, db=db,
                                    st=(e == 0 and ic == 0), sp=sp, e=e:
                                    t.matmul(
                                        p8[:, hc, :],
                                        swd[:, db, ic, hc * 128:(hc + 1) * 128],
                                        sa[:, e, ic, :], start=st, stop=sp,
                                        skip_group_check=True),
                                    wait=tuple(w),
                                    inc="tE" if (sp or last_of_unit) else None)
                                if sp:
                                    down_stop[hc] = tl
                                if last_of_unit:
                                    rel_wd[db] = tl
                    a_rel = down_stop[7]
                    evs = []
                    for hc in range(8):
                        tev = B.v(lambda v, hc=hc: v.tensor_tensor(
                            sxout[:, hc, :], p8[:, hc, :],
                            sxres[:, hc, :], ALU.add),
                            wait=(down_stop[hc], tres, rel_out[0]))
                        bank_rel[hc] = tev
                        evs.append(tev)
                    rel_res[0] = evs[-1]
                    rel_out[0] = B.dma("sync", fm_tile(dst_ap, sq, j),
                                       sxout[:], wait=tuple(evs))
                    if tile == 0 and dbg and tag == "f1":
                        if "dbg_a" in dbg:
                            B.dma("sync", dbg["dbg_a"], sa[:],
                                  wait=(a_rel,))
                        if "dbg_x1t" in dbg:
                            B.dma("sync", dbg["dbg_x1t"], sxout[:],
                                  wait=tuple(evs))
                B.flush()

        def emit_attn():
            B.phase()
            with ExitStack() as cl:
                el = cl.enter_context
                sx1t = el(nc.sbuf_tensor("a_x1", [128, 2, KC, TT], F32))
                sxl2 = el(nc.sbuf_tensor("a_ln", [128, 2, KC, TT], BF16))
                sstat = el(nc.sbuf_tensor("a_st", [128, 2, TT], F32))
                std_ = el(nc.sbuf_tensor("a_sd", [128, TT], F32))
                sK = el(nc.sbuf_tensor("a_K", [128, KC, S], BF16))
                sQ = el(nc.sbuf_tensor("a_Q", [128, KC, S], BF16))
                sV = el(nc.sbuf_tensor("a_V", [128, S // 128, C], BF16))
                swv = el(nc.sbuf_tensor("a_wv", [128, KC, TT], BF16))
                swx = el(nc.sbuf_tensor("a_wx", [128, 2, KC, 128], BF16))
                sPT = el(nc.sbuf_tensor("a_PT", [128, 8, TT], BF16))
                sO = el(nc.sbuf_tensor("a_O", [128, KC, TT], BF16))
                srinv = el(nc.sbuf_tensor("a_ri", [128, TT], F32))
                sln = el(nc.sbuf_tensor("a_sln", [128, TT], F32))
                sx2t = el(nc.sbuf_tensor("a_x2", [128, KC, TT], F32))
                p8 = el(nc.psum_tensor("a_p8", [128, 8, TT], F32))

                bank_rel = {b: None for b in range(8)}
                rel_x1 = [None, None]
                rel_ln2 = [None, None]
                t6_prev = [None]
                pending = [None]
                sqv_rel = [None]
                relw = [None, None]
                o_rel = [None]
                rinv_rel = [None]
                x2_rel = [None]
                kq_tok = {}
                v_tok = {}
                exp_toks = {}

                slot_state = {"i": 0}
                NSLOT = 8
                slot_rel = [None] * NSLOT

                def make_head(job, prow, b0, av, sm):
                    g, h, n = job
                    base = 64 * (h % 2)
                    oc = h // 2
                    pt = h % 2
                    slots = []
                    ex_toks = []
                    sum_tok = [None]

                    def s_chunk(c):
                        pb = b0 + (c % 2)
                        diag = c >= 4 * g
                        B.op("tensor", lambda t, c=c, pb=pb: t.matmul(
                            p8[:, pb, :],
                            sK[base:base + 64, oc, c * 128:(c + 1) * 128],
                            sQ[base:base + 64, oc, g * TT:(g + 1) * TT],
                            start=True, stop=not diag,
                            skip_group_check=True),
                            wait=(kq_tok[(1, oc, c // 4)],
                                  kq_tok[(0, oc, g)], bank_rel[pb]),
                            inc=None if diag else "tE")
                        if diag:
                            return B.t(lambda t, c=c, pb=pb: t.matmul(
                                p8[:, pb, :], idb[:],
                                cmneg[:, c - 4 * g, :],
                                start=False, stop=True,
                                skip_group_check=True))
                        return ("tE", B.val["tE"])

                    def s_pair(k):
                        t0 = s_chunk(2 * k)
                        t1 = s_chunk(2 * k + 1)
                        sl = slot_state["i"] % NSLOT
                        slot_state["i"] += 2
                        slots.append(sl)
                        return (t0, t1, sl)

                    def set_exp(tex):
                        bank_rel[b0] = tex
                        bank_rel[b0 + 1] = tex
                        ex_toks.append(tex)

                    def tails_pair(k):
                        sl = slots[k]
                        tex = ex_toks[k]
                        last = (2 * k + 1 == n - 1)
                        for ci in range(2):
                            c = 2 * k + ci
                            w = [tex]
                            if c == 0:
                                w.append(bank_rel[sm])
                            tsm = B.op("tensor", lambda t, sl=sl, ci=ci:
                                       t.matmul(
                                           p8[prow:prow + 64, sm, :],
                                           ones_bf[:, 0:64],
                                           sPT[:, sl + ci, :],
                                           start=(2 * k + ci == 0),
                                           stop=(2 * k + ci == n - 1),
                                           skip_group_check=True),
                                       wait=tuple(w),
                                       inc="tE" if 2 * k + ci == n - 1
                                       else None)
                            if 2 * k + ci == n - 1:
                                sum_tok[0] = tsm
                        tav = None
                        for ci in range(2):
                            c = 2 * k + ci
                            w = [tex, v_tok[c]]
                            if c == 0:
                                w.append(bank_rel[av])
                            tav = B.op("tensor", lambda t, sl=sl, ci=ci, c=c:
                                       t.matmul(
                                           p8[prow:prow + 64, av, :],
                                           sV[:, c, 64 * h:64 * h + 64],
                                           sPT[:, sl + ci, :],
                                           start=(c == 0), stop=(c == n - 1),
                                           skip_group_check=True),
                                       wait=tuple(w), inc="tE")
                            slot_rel[sl + ci] = tav
                        return tav

                    def finish(tav, trinv):
                        tev = B.v(lambda v: v.tensor_tensor(
                            sO[base:base + 64, oc, :],
                            p8[prow:prow + 64, av, :],
                            srinv[prow:prow + 64, :],
                            ALU.mult),
                            wait=(tav, trinv, o_rel[0]))
                        bank_rel[av] = tev
                        return tev

                    return (n, s_pair, tails_pair, finish, sum_tok,
                            set_exp, slots)

                def emit_pair(jobA, jobB, pp):
                    av = 4 + (pp % 2)
                    sm = 4 + ((pp + 1) % 2)
                    (nA, sA, tA, fA, stA, seA,
                     slotsA) = make_head(jobA, 0, 0, av, sm)
                    (nB, sB, tB, fB, stB, seB,
                     slotsB) = make_head(jobB, 64, 2, av, sm)
                    npair = nA // 2

                    def exp_k(k):
                        tA0, tA1, slA = sA(k)
                        tB0, tB1, slB = sB(k)
                        assert slB == slA + 2
                        tex = B.a(lambda a, slA=slA: a.activation(
                            sPT[:, slA:slA + 4, :], p8[:, 0:4, :],
                            AF.Exp, bias=0.0, scale=ATTN_SCALE),
                            wait=(tA0, tA1, tB0, tB1,
                                  slot_rel[slA], slot_rel[slA + 1],
                                  slot_rel[slA + 2], slot_rel[slA + 3]))
                        seA(tex)
                        seB(tex)

                    exp_k(0)
                    if pending[0] is not None:
                        pending[0]()
                        pending[0] = None
                    for k in range(1, npair):
                        exp_k(k)
                        tA(k - 1)
                        tB(k - 1)
                    tavA = tA(npair - 1)
                    tavB = tB(npair - 1)

                    def finish_pair(done_cb):
                        def run():
                            rw = [stA[0], stB[0]]
                            if rinv_rel[0] is not None:
                                rw.extend(rinv_rel[0])
                            tl1 = B.a(lambda a: a.activation(
                                sln[:], p8[:, sm, :], AF.Ln, bias=0.0,
                                scale=1.0), wait=tuple(rw))
                            bank_rel[sm] = tl1
                            tl2 = B.a(lambda a: a.activation(
                                srinv[:], sln[:], AF.Exp, bias=0.0,
                                scale=-1.0))
                            tevA = fA(tavA, tl2)
                            tevB = fB(tavB, tl2)
                            rinv_rel[0] = (tevA, tevB)
                            done_cb(tevA, tevB)
                        return run
                    return finish_pair

                def emit_wo(sq, g, otoks):
                    tr = B.dma("sync", sx2t[:], fm_tile(x1_ap, sq, g),
                               wait=(x2_rel[0],))
                    evs = []
                    for oc in range(KC):
                        sb = oc % 2
                        twl = B.dma("sync", swx[:, sb, :, :],
                                    cwq_ap[3].rearrange(
                                        "(kc p) o -> p kc o",
                                        p=128)[:, :, oc * 128:(oc + 1) * 128],
                                    wait=(relw[sb],))
                        pb = 6 + oc % 2
                        tpo = B.mm_group(
                            lambda pb=pb: p8[:, pb, :],
                            [(swx[:, sb, kc, :], sO[:, kc, :])
                             for kc in range(KC)],
                            wait_first=tuple([twl, bank_rel[pb]] + otoks))
                        relw[sb] = tpo
                        tev = B.v(lambda v, oc=oc, pb=pb: v.tensor_tensor(
                            sx2t[:, oc, :], sx2t[:, oc, :], p8[:, pb, :],
                            ALU.add), wait=(tpo, tr))
                        bank_rel[pb] = tev
                        evs.append(tev)
                    o_rel[0] = evs[-1]
                    x2_rel[0] = B.dma("sync", fm_tile(x2_ap, sq, g), sx2t[:],
                                      wait=tuple(evs))

                wv_rel = [None]
                for sq in range(BSEQ):
                    kq_tok.clear()
                    v_tok.clear()
                    def emit_ln(j):
                        bf = j % 2
                        tx = B.dma("sync", sx1t[:, bf, :, :],
                                   fm_tile(x1_ap, sq, j),
                                   wait=(rel_x1[bf],))
                        tb16 = B.v(lambda v, bf=bf: v.tensor_copy(
                            sxl2[:, bf, :, :], sx1t[:, bf, :, :]),
                            wait=(tx, rel_ln2[bf]))
                        sqv = sPT
                        tq2 = B.v(lambda v, bf=bf: v.tensor_tensor(
                            sPT[:, 0:KC, :], sxl2[:, bf, :, :],
                            sxl2[:, bf, :, :], ALU.mult),
                            wait=tuple([sqv_rel[0]] + list(slot_rel)))
                        return lambda: emit_ln_post(bf, tb16, tq2)

                    def emit_ln_post(bf, tb16, tq2):
                        tm = B.mm_group(
                            lambda: p8[:, 6, :],
                            [(ones_bf[:], sxl2[:, bf, kc, :])
                             for kc in range(KC)],
                            wait_first=(tb16, bank_rel[6]))
                        t1 = B.v(lambda v: v.tensor_single_scalar(
                            sstat[:, 0, :], p8[:, 6, :], 1.0 / C,
                            ALU.mult), wait=(tm,))
                        bank_rel[6] = t1
                        tmq = B.mm_group(
                            lambda: p8[:, 7, :],
                            [(ones_bf[:], sPT[:, kc, :])
                             for kc in range(KC)],
                            wait_first=(tq2, bank_rel[7]))
                        sqv_rel[0] = tmq
                        for _sl in range(NSLOT):
                            slot_rel[_sl] = tmq
                        t2 = B.v(lambda v: v.tensor_single_scalar(
                            sstat[:, 1, :], p8[:, 7, :], 1.0 / C,
                            ALU.mult), wait=(tmq,))
                        bank_rel[7] = t2
                        B.v(lambda v: v.scalar_tensor_tensor(
                            std_[:], sstat[:, 0, :], -1.0,
                            sstat[:, 0, :], ALU.mult, ALU.mult),
                            wait=(t6_prev[0],))
                        t4 = B.v(lambda v: v.tensor_add(
                            sstat[:, 1, :], sstat[:, 1, :], std_[:]))
                        B.a(lambda a: a.activation(
                            std_[:], sstat[:, 1, :], AF.Ln,
                            bias=seps[:], scale=1.0), wait=(t4,))
                        t6 = B.a(lambda a: a.activation(
                            sstat[:, 1, :], std_[:], AF.Exp,
                            bias=0.0, scale=-0.5))
                        t6_prev[0] = t6
                        tn = None
                        for kc in range(KC):
                            B.v(lambda v, bf=bf, kc=kc: v.tensor_sub(
                                std_[:], sx1t[:, bf, kc, :], sstat[:, 0, :]),
                                wait=(t6,) if kc == 0 else ())
                            tn = B.v(lambda v, bf=bf, kc=kc: v.tensor_tensor(
                                sxl2[:, bf, kc, :], std_[:], sstat[:, 1, :],
                                ALU.mult))
                        rel_x1[bf] = tn
                        return tn

                    def emit_qkv(j, tn, mid_post=None):
                        bf = j % 2
                        tn_new = None
                        for wi, dst in ((1, sK), (0, sQ)):
                            if wi == 0 and mid_post is not None:
                                tn_new = mid_post()
                            for oc in range(KC):
                                sb = oc % 2
                                twl = B.dma(
                                    "sync", swx[:, sb, :, :],
                                    cwq_ap[wi].rearrange(
                                        "(kc p) o -> p kc o",
                                        p=128)[:, :, oc * 128:(oc + 1) * 128],
                                    wait=(relw[sb],))
                                pb = 4 if wi == 1 else 5
                                tpk = B.mm_group(
                                    lambda pb=pb: p8[:, pb, :],
                                    [(swx[:, sb, kc, :], sxl2[:, bf, kc, :])
                                     for kc in range(KC)],
                                    wait_first=(twl, tn, bank_rel[pb]))
                                relw[sb] = tpk
                                rel_ln2[bf] = tpk
                                tcp = B.v(lambda v, dst=dst, oc=oc, j=j, pb=pb:
                                          v.tensor_copy(
                                              dst[:, oc,
                                                  j * TT:(j + 1) * TT],
                                              p8[:, pb, :]), wait=(tpk,))
                                bank_rel[pb] = tcp
                                kq_tok[(wi, oc, j)] = tcp
                        for og in range(2):
                            twv = B.dma(
                                "sync", swv[:],
                                cwq_ap[2].rearrange(
                                    "(kc p) o -> p kc o",
                                    p=128)[:, :, og * TT:(og + 1) * TT],
                                wait=(wv_rel[0],))
                            for tc in range(4):
                                tch = j * 4 + tc
                                pb = tc % 2
                                tpv = B.mm_group(
                                    lambda pb=pb: p8[:, pb, :],
                                    [(sxl2[:, bf, kc, tc * 128:(tc + 1) * 128],
                                      swv[:, kc, :])
                                     for kc in range(KC)],
                                    wait_first=(twv, tn, bank_rel[pb]))
                                rel_ln2[bf] = tpv
                                wv_rel[0] = tpv
                                tcv = B.v(lambda v, tch=tch, og=og, pb=pb:
                                          v.tensor_copy(
                                              sV[:, tch,
                                                 og * TT:(og + 1) * TT],
                                              p8[:, pb, :]), wait=(tpv,))
                                bank_rel[pb] = tcv
                                if og == 1:
                                    v_tok[tch] = tcv
                                else:
                                    v_tok[tch] = tcv
                        return tn_new

                    tn_prev = emit_ln(0)()
                    for j in range(1, 4):
                        post_new = emit_ln(j)
                        tn_prev = emit_qkv(j - 1, tn_prev, post_new)
                    emit_qkv(3, tn_prev)

                    o_done = {}

                    def mk_done(g):
                        def cb(ta, tb):
                            o_done.setdefault(g, []).extend([ta, tb])
                            if len(o_done[g]) == NH:
                                emit_wo(sq, g, o_done[g])
                        return cb

                    pp = 0
                    for g in range(4):
                        for h in range(0, NH, 2):
                            fin = emit_pair((g, h, 4 * g + 4),
                                            (g, h + 1, 4 * g + 4), pp)
                            pending[0] = fin(mk_done(g))
                            pp += 1
                    if pending[0] is not None:
                        pending[0]()
                        pending[0] = None
                B.flush()

        with ExitStack() as span1:
            w_exp = span1.enter_context(
                nc.sbuf_tensor("w_exp1", [12, TPC], BF16))
            (slog9, sl8, srwn, spre, wc_tok,
             tccin) = emit_route(span1, "r1", x_ap, True, 0)
            if _STOP != "P1":
                emit_gather("g1", slog9, sl8, srwn, spre, 0, w_exp, tccin)
                wexp_tok = ("vE", B.val["vE"])
                if _STOP != "P2":
                    emit_ffn("f1", x_ap, x1_ap, wexp_tok, w_exp, wc_tok)
        if _STOP is None:
            emit_attn()
            with ExitStack() as span2:
                w_exp = span2.enter_context(
                    nc.sbuf_tensor("w_exp2", [12, TPC], BF16))
                (slog9, sl8, srwn, spre, _wct,
                 tccin) = emit_route(span2, "r2", x2_ap, False, 1)
                emit_gather("g2", slog9, sl8, srwn, spre, 1, w_exp, tccin)
                wexp_tok = ("vE", B.val["vE"])
                emit_ffn("f2", x2_ap, out_ap, wexp_tok, w_exp)

        B.phase()
        for eng, q in B.queue_sem.items():
            B.ops[eng].append((((q, B.val[q]),), None, None, 0))
        B.flush()

    return nc


def _prep(inputs):
    f = np.float32

    def c(a):
        return np.ascontiguousarray(a, dtype=f)

    wqkvo = c(np.stack([inputs["L0_attn_Wq"].T, inputs["L0_attn_Wk"].T,
                        inputs["L0_attn_Wv"].T, inputs["L0_attn_Wo"].T]))
    dg = np.asarray(inputs["L0_ffn_dyn_gate_w"]).transpose(0, 2, 1)
    du = np.asarray(inputs["L0_ffn_dyn_up_w"]).transpose(0, 2, 1)
    dd = np.asarray(inputs["L0_ffn_dyn_down_w"]).transpose(0, 2, 1)
    sg_ = np.asarray(inputs["L0_ffn_sh_gate_w"])[0].T.reshape(
        1024, 4, 512).transpose(1, 0, 2)
    su_ = np.asarray(inputs["L0_ffn_sh_up_w"])[0].T.reshape(
        1024, 4, 512).transpose(1, 0, 2)
    sd_ = np.asarray(inputs["L0_ffn_sh_down_w"])[0].T.reshape(4, 512, 1024)
    wg = c(np.concatenate([dg, sg_], axis=0))
    wu = c(np.concatenate([du, su_], axis=0))
    wd = c(np.concatenate([dd, sd_], axis=0))
    gwt = c(inputs["L0_ffn_gate_w"].T)
    return wqkvo, wg, wu, wd, gwt


def kernel(**inputs):
    x = np.asarray(inputs["L0_attn_x"], dtype=np.float32)
    xT = np.ascontiguousarray(x.transpose(0, 2, 1))
    wqkvo, wg, wu, wd, gwt = _prep(inputs)
    nc = build_program()
    import ml_dtypes
    kco = np.zeros((12, NU, 128), dtype=ml_dtypes.bfloat16)
    for e in range(NU):
        kco[e, e, :] = 1.0
    iot = np.arange(128, dtype=np.float32)
    ktri = (iot[None, :] >= iot[:, None]).astype(np.float32)
    kid = np.eye(128, dtype=np.float32)
    in_maps = []
    for c in range(NCORE):
        in_maps.append({
            "x": np.ascontiguousarray(xT[2 * c:2 * c + 2]),
            "wqkvo": wqkvo, "wg": wg, "wu": wu, "wd": wd, "gwt": gwt,
            "cid": np.full((128, 1), float(c), dtype=np.float32),
            "kco": kco, "ktri": ktri, "kid": kid,
            "kpre": np.broadcast_to(
                (np.arange(8) < c)[:, None], (8, 128)
            ).astype(np.float32).copy(),
        })
    res = run_bass_kernel_spmd(nc, in_maps, core_ids=list(range(NCORE)))
    out = np.empty((16, S, C), dtype=np.float32)
    for c in range(NCORE):
        o = np.asarray(res.results[c]["out"], dtype=np.float32)
        out[2 * c:2 * c + 2] = o.transpose(0, 2, 1)
    globals()["_last_results"] = res
    return out
